# version 10
# baseline (speedup 1.0000x reference)
import numpy as np
from contextlib import ExitStack

import concourse.bass as bass
import concourse.mybir as mybir
from concourse.bass_utils import run_bass_kernel_spmd

F32 = mybir.dt.float32
BF16 = mybir.dt.bfloat16
ALU = mybir.AluOpType
AF = mybir.ActivationFunctionType
AXX = mybir.AxisListType.X

D = 4096
T = 2048
NH = 32
DH = 128
NC = D // 128
NT = T // 512
RANK = 1024
INTER = 11008
NI = INTER // 128
NR = RANK // 128
MODW = 6 * D
SCALE = 1.0 / float(np.sqrt(DH))
EPS = 1e-5

DEBUG = False

ENGS = ("g", "t", "v", "s", "y")


class Prog:

    def __init__(self, nc):
        self.nc = nc
        self.items = []
        self.ctr = {n: 0 for n in ("W", "X", "ST", "PE", "ACT", "DVE")}
        self.sems = {}

    def make_sems(self, ctx):
        for n in self.ctr:
            self.sems[n] = ctx.enter_context(self.nc.semaphore("sem" + n))

    def add(self, eng, fn, waits=(), incs=()):
        post = {}
        for (cn, n) in incs:
            self.ctr[cn] += n
            post[cn] = self.ctr[cn]
        ws = [(cn, v) for (cn, v) in waits if v is not None and v > 0]
        self.items.append((eng, fn, ws, list(incs)))
        return post

    _ECTR = {"v": "DVE", "s": "ACT", "t": "PE"}

    def addss(self, eng, fn, waits=()):
        cn = self._ECTR[eng]
        w = list(waits) + [(cn, self.ctr[cn])]
        return self.add(eng, fn, waits=w, incs=[(cn, 1)])

    def flush(self):
        items = self.items
        self.items = []
        if not items:
            return
        per = {e: [] for e in ENGS}
        for it in items:
            per[it[0]].append(it)
        nc = self.nc
        sems = self.sems
        with nc.Block() as blk:
            for ename in ENGS:
                ops = per[ename]
                if not ops:
                    continue
                dec = {"g": blk.gpsimd, "t": blk.tensor, "v": blk.vector,
                       "s": blk.scalar, "y": blk.sync}[ename]

                def body(eng, ops=ops):
                    waited = {}
                    for (_e, fn, waits, incs) in ops:
                        for (cn, val) in waits:
                            if waited.get(cn, -1) >= val:
                                continue
                            eng.wait_ge(sems[cn], val)
                            waited[cn] = val
                        inst = fn(eng)
                        for (cn, n) in incs:
                            inst.then_inc(sems[cn], n)
                dec(body)


def build_nc():
    nc = bass.Bass("TRN2", target_bir_lowering=False)

    x_fm = nc.dram_tensor("x_fm", [D, T], F32, kind="ExternalInput").ap()
    wq = nc.dram_tensor("wq", [D, D], F32, kind="ExternalInput").ap()
    wk = nc.dram_tensor("wk", [D, DH], F32, kind="ExternalInput").ap()
    wv = nc.dram_tensor("wv", [D, DH], F32, kind="ExternalInput").ap()
    wo = nc.dram_tensor("wo", [D, D], F32, kind="ExternalInput").ap()
    u1 = nc.dram_tensor("u1", [D, RANK], F32, kind="ExternalInput").ap()
    v1 = nc.dram_tensor("v1", [RANK, 2 * INTER], F32, kind="ExternalInput").ap()
    u2 = nc.dram_tensor("u2", [INTER, RANK], F32, kind="ExternalInput").ap()
    v2 = nc.dram_tensor("v2", [RANK, D], F32, kind="ExternalInput").ap()
    wmod_sl = nc.dram_tensor("wmod_sl", [D, 3 * RANK], F32, kind="ExternalInput").ap()
    b1r = nc.dram_tensor("b1r", [128, 2 * NI], F32, kind="ExternalInput").ap()
    b2r = nc.dram_tensor("b2r", [128, NC], F32, kind="ExternalInput").ap()
    c_all = nc.dram_tensor("c_all", [128, 8 * NC], F32, kind="ExternalInput").ap()
    ident_in = nc.dram_tensor("ident_in", [128, 128], F32, kind="ExternalInput").ap()
    ones_in = nc.dram_tensor("ones_in", [128, 128], F32, kind="ExternalInput").ap()
    triu_in = nc.dram_tensor("triu_in", [128, 128], F32, kind="ExternalInput").ap()

    out_fm = nc.dram_tensor("out_fm", [D, T], F32, kind="ExternalOutput").ap()
    wmod_t = nc.dram_tensor_like
    modp_d = nc.dram_tensor("modp_d", [8, 3 * RANK], F32)
    modg_t = nc.dram_tensor("modg_d", [64, 3 * RANK], F32)
    modg_d = modg_t.ap()
    skind = "ExternalOutput" if DEBUG else "Internal"
    x1_d = nc.dram_tensor("x1_d", [D, T], F32, kind=skind).ap()
    attn_d = nc.dram_tensor("attn_d", [D, T], BF16, kind=skind).ap()
    dbg = {}
    if DEBUG:
        for nm, shape, dt in [("dbg_mod", [128, 192], F32), ("dbg_h1", [D, T], BF16),
                              ("dbg_k", [128, T], BF16), ("dbg_v", [128, T], BF16),
                              ("dbg_q0", [128, T], BF16), ("dbg_h2", [D, T], BF16),
                              ("dbg_p", [128, NR * T], BF16), ("dbg_z", [128, NR * T], BF16),
                              ("dbg_scal", [1, 16], F32)]:
            dbg[nm] = nc.dram_tensor(nm, shape, dt, kind="ExternalOutput").ap()

    MOD_SA, MOD_C1A, MOD_GA, MOD_SB, MOD_C1B, MOD_GB = 0, 32, 64, 96, 128, 160

    nc.cache_partition_id()
    ctx0 = ExitStack()
    with ctx0:
        P = Prog(nc)
        P.make_sems(ctx0)
        sbt = lambda name, shape, dt, **kw: ctx0.enter_context(
            nc.sbuf_tensor(name, shape, dt, **kw))

        identb = sbt("identb", [128, 128], BF16)
        onesb = sbt("onesb", [128, 128], BF16)
        triub = sbt("triub", [128, 128], BF16)
        one11 = sbt("one11", [1, 1], BF16)
        b1_s = sbt("b1_s", [128, 2 * NI], F32)
        b2_s = sbt("b2_s", [128, NC], F32)
        modv = sbt("modv", [128, 192], F32)
        scal = sbt("scal", [1, 16], F32)
        scalb = sbt("scalb", [1, 2], BF16)
        negb_s = sbt("negb_s", [128, 2], F32)
        b2gb = sbt("b2gb", [128, NC], F32)
        eps_s = sbt("eps_s", [128, 1], F32)

        ctxA = ExitStack()
        silc8 = ctxA.enter_context(nc.sbuf_tensor("silc8", [128, 8 * NC], F32))
        mod_row = ctxA.enter_context(nc.sbuf_tensor("mod_row", [1, MODW], F32))
        modp_sb = ctxA.enter_context(nc.sbuf_tensor("modp_sb", [8, 3 * RANK], F32))

        P.add("g", lambda g: g.dma_start(out=identb[:, :], in_=ident_in), incs=[("W", 16)])
        P.add("g", lambda g: g.dma_start(out=onesb[:, :], in_=ones_in), incs=[("W", 16)])
        P.add("g", lambda g: g.dma_start(out=triub[:, :], in_=triu_in), incs=[("W", 16)])
        P.add("g", lambda g: g.dma_start(out=b1_s[:, :], in_=b1r), incs=[("W", 16)])
        P.add("g", lambda g: g.dma_start(out=b2_s[:, :], in_=b2r), incs=[("W", 16)])
        cdma = P.add("g", lambda g: g.dma_start(out=silc8[:, :], in_=c_all), incs=[("W", 16)])
        P.add("v", lambda v: v.memset(one11[:, :], 1.0))
        P.add("v", lambda v: v.memset(scal[:, :], 0.0))
        P.add("v", lambda v: v.memset(eps_s[:, :], EPS))
        sact = P.add("s", lambda s: s.activation(silc8[:, :], silc8[:, :], AF.Silu),
                     waits=[("W", cdma["W"])], incs=[("ACT", 1)])

        wmb = [ctxA.enter_context(nc.sbuf_tensor(f"wmb{i}", [128, NC * 256], F32))
               for i in range(2)]
        pmv = [ctxA.enter_context(nc.psum_tensor(f"pmv{i}", [8, 256], F32))
               for i in range(2)]
        NBLK = 3 * RANK // 256
        mv_pe = [None] * NBLK
        mv_ev = [None] * NBLK
        for n in range(NBLK):
            b = n % 2
            last = None
            for q in range(NC):
                def fdma(y, n=n, q=q, b=b):
                    src = wmod_sl[q * 128:(q + 1) * 128, n * 256:(n + 1) * 256]
                    dst = wmb[b][:, q * 256:(q + 1) * 256]
                    return y.dma_start(out=dst, in_=src)
                w = [("PE", mv_pe[n - 2]["PE"])] if (n >= 2 and q == 0) else []
                last = P.add("g", fdma, waits=w, incs=[("W", 16)])

            def fmm(t, b=b):
                mm = None
                for j in range(NC):
                    mm = t.matmul(pmv[b][0:8, :], silc8[:, j * 8:(j + 1) * 8],
                                  wmb[b][:, j * 256:(j + 1) * 256],
                                  start=(j == 0), stop=(j == NC - 1))
                return mm
            w2 = [("W", last["W"]), ("ACT", sact["ACT"])]
            if n >= 1:
                w2.append(("DVE", mv_ev[n - 1]["DVE"]))
            mv_pe[n] = P.add("t", fmm, waits=w2, incs=[("PE", 1)])
            mv_ev[n] = P.addss("v", lambda v, n=n, b=b: v.tensor_copy(
                modp_sb[0:8, n * 256:(n + 1) * 256], pmv[b][0:8, :]),
                waits=[("PE", mv_pe[n]["PE"])])

        mst = P.add("y", lambda y: y.dma_start(out=modp_d.ap(), in_=modp_sb[0:8, :]),
                    waits=[("DVE", mv_ev[NBLK - 1]["DVE"])], incs=[("ST", 16)])
        cc = P.add("g", lambda g: g.collective_compute(
            "AllGather", ALU.bypass, replica_groups=[list(range(8))],
            ins=[modp_d.ap().opt()], outs=[modg_d.opt()]),
            waits=[("ST", mst["ST"])], incs=[("W", 1)])

        def fext(g):
            pid = g.partition_id()
            src = bass.AP(modg_t, pid * (3 * RANK), [[8 * 3 * RANK, 8], [1, 3 * RANK]])
            dst = mod_row[0:1, :].rearrange("a (s f) -> a s f", s=8)
            return g.dma_start(out=dst, in_=src)
        ext = P.add("g", fext, waits=[("W", cc["W"])], incs=[("W", 16)])

        pcol = ctxA.enter_context(nc.psum_tensor("pcol", [128, 192], F32))
        sgn = ctxA.enter_context(nc.sbuf_tensor("sgn", [128, 32], F32))
        tmq = ctxA.enter_context(nc.sbuf_tensor("tmq", [128, 32], F32))
        one11f = ctxA.enter_context(nc.sbuf_tensor("one11f", [1, 1], F32))
        o1m = P.add("v", lambda v: v.memset(one11f[:, :], 1.0), incs=[("DVE", 1)])

        def fcols(t):
            mm = None
            for col in range(192):
                g_, j = divmod(col, 32)
                mm = t.matmul(pcol[:, col:col + 1],
                              mod_row[0:1, g_ * D + j * 128: g_ * D + (j + 1) * 128],
                              one11f[0:1, 0:1], start=(col == 0), stop=(col == 191))
            return mm
        pc = P.add("t", fcols, waits=[("W", ext["W"]), ("DVE", o1m["DVE"])], incs=[("PE", 1)])
        pcv = ctxA.enter_context(nc.sbuf_tensor("pcv", [128, 192], F32))
        P.addss("v", lambda v: v.tensor_copy(pcv[:, :], pcol[:, :]),
                waits=[("PE", pc["PE"])])
        P.addss("v", lambda v: v.tensor_copy(modv[:, MOD_SA:MOD_SA + 32], pcv[:, 0:32]))
        P.addss("v", lambda v: v.tensor_copy(modv[:, MOD_SB:MOD_SB + 32], pcv[:, 96:128]))
        P.addss("v", lambda v: v.tensor_single_scalar(modv[:, MOD_C1A:MOD_C1A + 32],
                                                      pcv[:, 32:64], 1.0, ALU.add))
        P.addss("v", lambda v: v.tensor_single_scalar(modv[:, MOD_C1B:MOD_C1B + 32],
                                                      pcv[:, 128:160], 1.0, ALU.add))
        modv_done = None
        for src0, dst in ((64, MOD_GA), (160, MOD_GB)):
            P.addss("v", lambda v, src0=src0: v.tensor_single_scalar(
                sgn[:, :], pcv[:, src0:src0 + 32], 0.0, ALU.is_ge))
            P.addss("v", lambda v: v.tensor_scalar(sgn[:, :], sgn[:, :], 2.0, -1.0,
                                                   ALU.mult, ALU.add))
            P.addss("v", lambda v, src0=src0: v.tensor_mul(tmq[:, :], pcv[:, src0:src0 + 32],
                                                           sgn[:, :]))
            P.addss("v", lambda v: v.tensor_scalar_max(tmq[:, :], tmq[:, :], 0.25))
            modv_done = P.addss("v", lambda v, dst=dst: v.tensor_mul(
                modv[:, dst:dst + 32], tmq[:, :], sgn[:, :]))
        P.addss("v", lambda v: v.tensor_mul(b2gb[:, :], b2_s[:, :],
                                            modv[:, MOD_GB:MOD_GB + 32]),
                waits=[("DVE", modv_done["DVE"])])
        if DEBUG:
            P.add("y", lambda y: y.dma_start(out=dbg["dbg_mod"], in_=modv[:, :]),
                  waits=[("DVE", modv_done["DVE"])], incs=[("ST", 16)])
        P.flush()
        ctxA.close()

        _uid = [0]

        def ln_phase(x_src, h_dst, c1_off, s_off, ready=()):
            _uid[0] += 1
            u = _uid[0]
            ctxB = ExitStack()
            with ctxB:
                sb2 = lambda name, shape, dt: ctxB.enter_context(
                    nc.sbuf_tensor(f"{name}_{u}", shape, dt))
                xbb = [sb2(f"lnxb{i}", [128, T], BF16) for i in range(2)]
                x2b = [sb2(f"lnx2{i}", [128, T], BF16) for i in range(2)]
                m_s = sb2("ln_m", [128, T], F32)
                r_s = sb2("ln_r", [128, T], F32)
                var_s = sb2("ln_var", [128, T], F32)
                xf = [sb2(f"lnxf{i}", [128, T], F32) for i in range(2)]
                uu = sb2("ln_uu", [128, T], F32)
                pSx = ctxB.enter_context(nc.psum_tensor(f"pSx_{u}", [128, T], F32))
                pSx2 = ctxB.enter_context(nc.psum_tensor(f"pSx2_{u}", [128, T], F32))

                mmj = [None] * NC
                for j in range(NC):
                    b = j % 2
                    w = [("PE", mmj[j - 2]["PE"])] if j >= 2 else list(ready)
                    dj = P.add("g", lambda g, j=j, b=b: g.dma_start(
                        out=xbb[b][:, :], in_=x_src[j * 128:(j + 1) * 128, :]),
                        waits=w, incs=[("W", 16)])
                    w = [("W", dj["W"])]
                    if j >= 2:
                        w.append(("PE", mmj[j - 2]["PE"]))
                    sq = P.add("v", lambda v, b=b: v.tensor_mul(
                        x2b[b][:, :], xbb[b][:, :], xbb[b][:, :]),
                        waits=w, incs=[("DVE", 1)])

                    def fstat(t, j=j, b=b):
                        mm = None
                        for tt in range(NT):
                            sl = slice(tt * 512, (tt + 1) * 512)
                            t.matmul(pSx[:, sl], onesb[:, :], xbb[b][:, sl],
                                     start=(j == 0), stop=(j == NC - 1))
                            mm = t.matmul(pSx2[:, sl], onesb[:, :], x2b[b][:, sl],
                                          start=(j == 0), stop=(j == NC - 1))
                        return mm
                    mmj[j] = P.add("t", fstat, waits=[("DVE", sq["DVE"])], incs=[("PE", 1)])

                wst = [("PE", mmj[NC - 1]["PE"])]
                P.add("v", lambda v: v.tensor_single_scalar(m_s[:, :], pSx[:, :],
                                                            1.0 / D, ALU.mult), waits=wst)
                P.add("v", lambda v: v.tensor_mul(var_s[:, :], m_s[:, :], m_s[:, :]))
                dv = P.add("v", lambda v: v.scalar_tensor_tensor(
                    var_s[:, :], pSx2[:, :], 1.0 / D, var_s[:, :], ALU.mult, ALU.subtract),
                    incs=[("DVE", 1)])
                P.add("s", lambda s: s.activation(var_s[:, :], var_s[:, :], AF.Ln, bias=eps_s[:, 0:1]),
                      waits=[("DVE", dv["DVE"])])
                ract = P.add("s", lambda s: s.activation(r_s[:, :], var_s[:, :], AF.Exp,
                                                         scale=-0.5), incs=[("ACT", 1)])

                dvj = [None] * NC
                actj = [None] * NC
                for j in range(NC):
                    b = j % 2
                    w = [("DVE", dvj[j - 2]["DVE"])] if j >= 2 else list(ready)
                    xdj = P.add("y", lambda y, j=j, b=b: y.dma_start(
                        out=xf[b][:, :], in_=x_src[j * 128:(j + 1) * 128, :]),
                        waits=w, incs=[("X", 16)])
                    w = [("X", xdj["X"]), ("ACT", ract["ACT"])]
                    if j >= 1:
                        w.append(("ACT", actj[j - 1]["ACT"]))
                    P.add("v", lambda v, b=b: v.tensor_sub(uu[:, :], xf[b][:, :], m_s[:, :]),
                          waits=w)
                    dvj[j] = P.add("v", lambda v, j=j: v.scalar_tensor_tensor(
                        uu[:, :], uu[:, :], modv[:, c1_off + j:c1_off + j + 1], r_s[:, :],
                        ALU.mult, ALU.mult), incs=[("DVE", 1)])
                    actj[j] = P.add("s", lambda s, j=j: s.activation(
                        h_dst[:, j * T:(j + 1) * T], uu[:, :], AF.Identity,
                        bias=modv[:, s_off + j:s_off + j + 1]),
                        waits=[("DVE", dvj[j]["DVE"])], incs=[("ACT", 1)])
                P.flush()
            return actj[NC - 1]["ACT"]

        ctxH1 = ExitStack()
        h1 = ctxH1.enter_context(nc.sbuf_tensor("h1", [128, NC * T], BF16))
        h1_done = ln_phase(x_fm, h1, MOD_C1A, MOD_SA)
        if DEBUG:
            for j in range(NC):
                P.add("y", lambda y, j=j: y.dma_start(
                    out=dbg["dbg_h1"][j * 128:(j + 1) * 128, :], in_=h1[:, j * T:(j + 1) * T]),
                    waits=[("ACT", h1_done)], incs=[("ST", 16)])
            P.flush()

        ctxC = ExitStack()
        with ctxC:
            sb3 = lambda name, shape, dt: ctxC.enter_context(nc.sbuf_tensor(name, shape, dt))
            k_fm = sb3("k_fm", [128, T], BF16)
            vfm = sb3("vfm", [128, T], BF16)
            v_tm = sb3("v_tm", [128, T], BF16)
            k2b = sb3("k2b", [128, T], BF16)
            wqb = [sb3(f"wqb{i}", [128, NC * 128], BF16) for i in range(2)]
            q_h = [sb3(f"q_h{i}", [128, T], BF16) for i in range(2)]
            q2b = sb3("q2b", [128, T], BF16)
            ptb = [sb3(f"ptb{i}", [128, 512], BF16) for i in range(2)]
            recip = [sb3(f"recip{i}", [128, 512], F32) for i in range(2)]
            lnden = [sb3(f"lnden{i}", [128, 512], F32) for i in range(2)]
            atb = [sb3(f"atb{i}", [128, T], BF16) for i in range(2)]

            pq = [ctxC.enter_context(nc.psum_tensor(f"pq{i}", [128, 512], F32)) for i in range(2)]
            pS = [ctxC.enter_context(nc.psum_tensor(f"pS{i}", [128, 512], F32)) for i in range(2)]
            pPV = [ctxC.enter_context(nc.psum_tensor(f"pPV{i}", [128, 512], F32)) for i in range(2)]
            pDen = [ctxC.enter_context(nc.psum_tensor(f"pDen{i}", [128, 512], F32)) for i in range(2)]
            pM = pq[0]
            pTr = pS[0][:, 0:64].bitcast(BF16)

            last_pq_ev = {}
            last_pS_exp = {}
            last_ptb_den = {}

            kv_dma = [None, None]
            for which, wsrc in ((0, wk), (1, wv)):
                def fkv(g, which=which, wsrc=wsrc):
                    dst = wqb[which][:, 0:NC * DH].rearrange("p (c f) -> p c f", c=NC)
                    return g.dma_start(out=dst, in_=wsrc.rearrange("(c p) f -> p c f", p=128))
                kv_dma[which] = P.add("g", fkv, incs=[("W", 16)])

            kv_idx = 0
            kv_last_ev = None
            kv_last_pe = {}
            for which, dst in ((0, k_fm), (1, vfm)):
                for tt in range(NT):
                    par = kv_idx % 2

                    def fproj(t, which=which, tt=tt, par=par):
                        mm = None
                        for j in range(NC):
                            mm = t.matmul(pq[par][:, :],
                                          wqb[which][:, j * 128:(j + 1) * 128],
                                          h1[:, j * T + tt * 512: j * T + (tt + 1) * 512],
                                          start=(j == 0), stop=(j == NC - 1))
                        return mm
                    w = [("W", kv_dma[which]["W"]), ("ACT", h1_done)]
                    if par in last_pq_ev:
                        w.append(("DVE", last_pq_ev[par]["DVE"]))
                    mm = P.add("t", fproj, waits=w, incs=[("PE", 1)])
                    kv_last_pe[which] = mm
                    kv_last_ev = P.add("v", lambda v, dst=dst, tt=tt, par=par: v.tensor_copy(
                        dst[:, tt * 512:(tt + 1) * 512], pq[par][:, :]),
                        waits=[("PE", mm["PE"])], incs=[("DVE", 1)])
                    last_pq_ev[par] = kv_last_ev
                    kv_idx += 1
                if which == 0:
                    k_done = kv_last_ev
            v_done = kv_last_ev

            k2d = P.add("v", lambda v: v.tensor_mul(k2b[:, :], k_fm[:, :], k_fm[:, :]),
                        waits=[("DVE", k_done["DVE"])], incs=[("DVE", 1)])
            vt_prev = None
            for c in range(16):
                w = [("DVE", v_done["DVE"])]
                if vt_prev is not None:
                    w.append(("DVE", vt_prev["DVE"]))
                mm = P.add("t", lambda t, c=c: t.transpose(
                    pTr, vfm[:, c * 128:(c + 1) * 128], identb[:, :]),
                    waits=w, incs=[("PE", 1)])
                vt_prev = P.add("v", lambda v, c=c: v.tensor_copy(
                    v_tm[:, c * 128:(c + 1) * 128], pTr),
                    waits=[("PE", mm["PE"])], incs=[("DVE", 1)])
            kred_prev = None
            for tt in range(NT):
                w = [("DVE", k2d["DVE"]), ("DVE", v_done["DVE"])]
                if kred_prev is not None:
                    w.append(("DVE", kred_prev["DVE"]))
                mm = P.add("t", lambda t, tt=tt: t.matmul(
                    pM[0:1, 0:512], onesb[:, 0:1], k2b[:, tt * 512:(tt + 1) * 512],
                    start=True, stop=True), waits=w, incs=[("PE", 1)])
                kred_prev = P.addss("v", lambda v, tt=tt: v.reduce_max(
                    scal[0:1, tt:tt + 1], pM[0:1, 0:512], axis=AXX),
                    waits=[("PE", mm["PE"])])
            kn2m = P.addss("v", lambda v: v.reduce_max(scal[0:1, 8:9], scal[0:1, 0:4], axis=AXX),
                           waits=[("DVE", kred_prev["DVE"])])
            P.flush()
            if DEBUG:
                P.add("y", lambda y: y.dma_start(out=dbg["dbg_k"], in_=k_fm[:, :]),
                      incs=[("ST", 16)])
                P.add("y", lambda y: y.dma_start(out=dbg["dbg_v"], in_=v_tm[:, :]),
                      incs=[("ST", 16)])
                P.flush()

            at_store = [None] * NH
            wq_pe_last = {0: kv_last_pe[0], 1: kv_last_pe[1]}
            pv_done_prev = None
            last_pv_ev = {}
            last_den_rd = {}
            nq_bank_rd = {}
            last_rc2 = None
            qdma = [None] * NH

            def emit_wq_dma(h):
                bq = h % 2
                w = []
                if bq in wq_pe_last:
                    w = [("PE", wq_pe_last[bq]["PE"])]
                last = None
                for q_ in range(2):
                    def fwq(g, h=h, q_=q_, bq=bq):
                        src = wq[q_ * 2048:(q_ + 1) * 2048, h * 128:(h + 1) * 128]
                        src = src.rearrange("(c p) f -> p c f", p=128)
                        dst = wqb[bq][:, q_ * 16 * 128:(q_ + 1) * 16 * 128]
                        dst = dst.rearrange("p (c f) -> p c f", c=16)
                        return g.dma_start(out=dst, in_=src)
                    last = P.add("g", fwq, waits=w if q_ == 0 else [], incs=[("W", 16)])
                qdma[h] = last

            def emit_q_group(h, tt):
                hp = h % 2
                par = tt % 2
                bq = h % 2

                def fq(t, tt=tt, bq=bq, par=par):
                    mm = None
                    for j in range(NC):
                        mm = t.matmul(pq[par][:, :], wqb[bq][:, j * 128:(j + 1) * 128],
                                      h1[:, j * T + tt * 512: j * T + (tt + 1) * 512],
                                      start=(j == 0), stop=(j == NC - 1))
                    return mm
                w = [("W", qdma[h]["W"])]
                if par in last_pq_ev:
                    w.append(("DVE", last_pq_ev[par]["DVE"]))
                if ("pq", par) in nq_bank_rd:
                    w.append(("DVE", nq_bank_rd[("pq", par)]["DVE"]))
                mmq = P.add("t", fq, waits=w, incs=[("PE", 1)])
                if tt == NT - 1:
                    wq_pe_last[bq] = mmq
                P.add("v", lambda v, tt=tt, par=par, hp=hp: v.tensor_copy(
                    q_h[hp][:, tt * 512:(tt + 1) * 512], pq[par][:, :]),
                    waits=[("PE", mmq["PE"])])
                qe = P.add("v", lambda v, tt=tt, hp=hp: v.tensor_mul(
                    q2b[:, tt * 512:(tt + 1) * 512],
                    q_h[hp][:, tt * 512:(tt + 1) * 512],
                    q_h[hp][:, tt * 512:(tt + 1) * 512]),
                    incs=[("DVE", 1)])
                last_pq_ev[par] = qe
                return qe

            def emit_bound_chain(h, qev_last):
                hp = h % 2
                banks = [("pq", 0, pq[0]), ("pq", 1, pq[1]),
                         ("pPV", 0, pPV[0]), ("pDen", 0, pDen[0])]
                nred = []
                for tt in range(NT):
                    kind, idx, pt = banks[tt]
                    w = [("DVE", qev_last["DVE"])]
                    if kind == "pPV" and idx in last_pv_ev:
                        w.append(("DVE", last_pv_ev[idx]["DVE"]))
                    if kind == "pDen" and idx in last_den_rd:
                        w.append(("ACT", last_den_rd[idx]["ACT"]))
                    mm = P.add("t", lambda t, tt=tt, pt=pt: t.matmul(
                        pt[0:1, 0:512], onesb[:, 0:1], q2b[:, tt * 512:(tt + 1) * 512],
                        start=True, stop=True), waits=w, incs=[("PE", 1)])
                    rd = P.addss("v", lambda v, tt=tt, pt=pt: v.reduce_max(
                        scal[0:1, 4 + tt:5 + tt], pt[0:1, 0:512], axis=AXX),
                        waits=[("PE", mm["PE"])])
                    nq_bank_rd[(kind, idx)] = rd
                    nred.append(rd)
                P.addss("v", lambda v: v.reduce_max(scal[0:1, 9:10], scal[0:1, 4:8], axis=AXX),
                        waits=[("DVE", nred[-1]["DVE"]), ("DVE", kn2m["DVE"])])
                dmu = P.addss("v", lambda v: v.tensor_mul(scal[0:1, 10:11], scal[0:1, 8:9],
                                                          scal[0:1, 9:10]))
                asq = P.addss("s", lambda s: s.activation(scal[0:1, 11:12], scal[0:1, 10:11],
                                                          AF.Sqrt, scale=1.0 / DH),
                              waits=[("DVE", dmu["DVE"])])
                dng = P.addss("v", lambda v, hp=hp: v.tensor_single_scalar(
                    scalb[0:1, hp:hp + 1], scal[0:1, 11:12], -1.0, ALU.mult),
                    waits=[("ACT", asq["ACT"])])
                bmm = P.add("t", lambda t, hp=hp: t.matmul(
                    pq[0][:, 0:1], onesb[0:1, 0:128], scalb[0:1, hp:hp + 1],
                    start=True, stop=True),
                    waits=[("DVE", dng["DVE"])], incs=[("PE", 1)])
                nbc = P.addss("v", lambda v, hp=hp: v.tensor_copy(
                    negb_s[:, hp:hp + 1], pq[0][:, 0:1]),
                    waits=[("PE", bmm["PE"])])
                last_pq_ev[0] = nbc
                return nbc

            def make_slots(hn, qev_box):
                def s0():
                    emit_q_group(hn, 0)
                def s1():
                    emit_q_group(hn, 1)
                def s2():
                    emit_q_group(hn, 2)
                    qev_box[0] = emit_q_group(hn, 3)
                def s3():
                    nbc_box[hn] = emit_bound_chain(hn, qev_box[0])
                return [s0, s1, s2, s3]

            nbc_box = {}
            qev_box = [None]

            emit_wq_dma(0)
            emit_q_group(0, 0)
            emit_q_group(0, 1)
            emit_q_group(0, 2)
            qe0 = emit_q_group(0, 3)
            nbc_box[0] = emit_bound_chain(0, qe0)

            for h in range(NH):
                hp = h % 2
                nbc = nbc_box[h]
                if h + 1 < NH:
                    emit_wq_dma(h + 1)
                    slots = make_slots(h + 1, qev_box)
                else:
                    slots = [lambda: None] * 4

                for tt in range(NT):
                    cmax = 4 * (tt + 1)
                    vpar = tt % 2
                    expq = {}
                    den_q = {}

                    def emit_s_exp(c):
                        qlo = max(tt * 512, c * 128)
                        N = (tt + 1) * 512 - qlo
                        spar = c % 2
                        w = []
                        if spar in last_pS_exp:
                            w.append(("ACT", last_pS_exp[spar]["ACT"]))
                        if ("pS", spar) in nq_bank_rd:
                            w.append(("DVE", nq_bank_rd.pop(("pS", spar))["DVE"]))
                        smm = P.add("t", lambda t, c=c, N=N, qlo=qlo, spar=spar,
                                    tt=tt, hp=hp:
                                    t.matmul(pS[spar][:, 0:N],
                                             k_fm[:, c * 128:(c + 1) * 128],
                                             q_h[hp][:, qlo:(tt + 1) * 512],
                                             start=True, stop=True),
                                    waits=w, incs=[("PE", 1)])
                        w = [("PE", smm["PE"]), ("DVE", nbc["DVE"])]
                        if spar in last_ptb_den:
                            w.append(("PE", last_ptb_den[spar]["PE"]))
                        ex = P.add("s", lambda s, N=N, spar=spar, hp=hp: s.activation(
                            ptb[spar][:, 0:N], pS[spar][:, 0:N], AF.Exp,
                            bias=negb_s[:, hp:hp + 1], scale=SCALE),
                            waits=w, incs=[("ACT", 1)])
                        last_pS_exp[spar] = ex
                        expq[c] = {"ACT": ex["ACT"], "DVE": None}
                        if c >= 4 * tt:
                            mk = P.add("v", lambda v, spar=spar: v.tensor_mul(
                                ptb[spar][:, 0:128], ptb[spar][:, 0:128], triub[:, :]),
                                waits=[("ACT", ex["ACT"])], incs=[("DVE", 1)])
                            expq[c]["DVE"] = mk["DVE"]

                    def emit_pv_den(cc):
                        qlo2 = max(tt * 512, cc * 128)
                        N2 = (tt + 1) * 512 - qlo2
                        off2 = qlo2 - tt * 512
                        e2 = expq[cc]
                        w = [("ACT", e2["ACT"])]
                        if e2["DVE"] is not None:
                            w = [("DVE", e2["DVE"])]
                        if cc == 0:
                            if vpar in last_pv_ev:
                                w.append(("DVE", last_pv_ev[vpar]["DVE"]))
                            if ("pPV", vpar) in nq_bank_rd:
                                w.append(("DVE", nq_bank_rd.pop(("pPV", vpar))["DVE"]))
                        P.add("t", lambda t, cc=cc, N2=N2, off2=off2, cmax=cmax, vpar=vpar:
                              t.matmul(pPV[vpar][:, off2:off2 + N2],
                                       v_tm[:, cc * 128:(cc + 1) * 128],
                                       ptb[cc % 2][:, 0:N2],
                                       start=(cc == 0), stop=(cc == cmax - 1)),
                              waits=w)
                        wd = []
                        if cc == 0:
                            if vpar in last_den_rd:
                                wd.append(("ACT", last_den_rd[vpar]["ACT"]))
                            if ("pDen", vpar) in nq_bank_rd:
                                wd.append(("DVE", nq_bank_rd.pop(("pDen", vpar))["DVE"]))
                        dn = P.add("t", lambda t, cc=cc, N2=N2, off2=off2, cmax=cmax, vpar=vpar:
                                   t.matmul(pDen[vpar][:, off2:off2 + N2], onesb[:, :],
                                            ptb[cc % 2][:, 0:N2],
                                            start=(cc == 0), stop=(cc == cmax - 1)),
                                   waits=wd, incs=[("PE", 1)])
                        last_ptb_den[cc % 2] = dn
                        den_q[cc] = dn

                    for c in range(cmax):
                        emit_s_exp(c)
                        if c >= 1:
                            emit_pv_den(c - 1)
                    emit_pv_den(cmax - 1)

                    w = [("PE", den_q[cmax - 1]["PE"])]
                    if vpar in last_pv_ev:
                        w.append(("DVE", last_pv_ev[vpar]["DVE"]))
                    rc1 = P.add("s", lambda s, vpar=vpar: s.activation(
                        lnden[vpar][:, :], pDen[vpar][:, 0:512], AF.Ln),
                        waits=w, incs=[("ACT", 1)])
                    last_den_rd[vpar] = rc1
                    rc2 = P.add("s", lambda s, vpar=vpar: s.activation(
                        recip[vpar][:, :], lnden[vpar][:, :], AF.Exp,
                        scale=-1.0), incs=[("ACT", 1)])
                    last_rc2 = rc2
                    w = [("ACT", rc2["ACT"])]
                    if h >= 2 and tt == 0:
                        w.append(("ST", at_store[h - 2]["ST"]))
                    pv_done_prev = P.add("v", lambda v, tt=tt, hp=hp, vpar=vpar: v.tensor_mul(
                        atb[hp][:, tt * 512:(tt + 1) * 512], pPV[vpar][:, 0:512],
                        recip[vpar][:, :]),
                        waits=w, incs=[("DVE", 1)])
                    last_pv_ev[vpar] = pv_done_prev

                    slots[tt]()

                at_store[h] = P.add("y", lambda y, h=h, hp=hp: y.dma_start(
                    out=attn_d[h * 128:(h + 1) * 128, :], in_=atb[hp][:, :]),
                    waits=[("DVE", pv_done_prev["DVE"])], incs=[("ST", 16)])
            P.flush()
        ctxC.close()
        ctxH1.close()

        ctxD = ExitStack()
        with ctxD:
            sb4 = lambda name, shape, dt: ctxD.enter_context(nc.sbuf_tensor(name, shape, dt))
            attn_sb = sb4("attn_sb", [128, NC * T], BF16)
            wob = [sb4(f"wob{i}", [128, NC * 128], BF16) for i in range(2)]
            xfb = [sb4(f"xfb{i}", [128, T], F32) for i in range(2)]
            x1b = [sb4(f"x1b{i}", [128, T], F32) for i in range(2)]
            pw = [ctxD.enter_context(nc.psum_tensor(f"pw{i}", [128, 512], F32))
                  for i in range(2)]

            at0 = at_store[NH - 1]["ST"]
            arel = None
            for q_ in range(4):
                def fat(y, q_=q_):
                    src = attn_d[q_ * 1024:(q_ + 1) * 1024, :]
                    src = src.rearrange("(c p) t -> p c t", p=128)
                    dst = attn_sb[:, q_ * 8 * T:(q_ + 1) * 8 * T]
                    dst = dst.rearrange("p (c t) -> p c t", c=8)
                    return y.dma_start(out=dst, in_=src)
                arel = P.add("y", fat, waits=[("ST", at0)] if q_ == 0 else [],
                             incs=[("X", 16)])

            wo_last_pe = {}
            pw_last_ev = {}
            xf_last_ev = {}
            x1_store = [None] * NC
            for f in range(NC):
                bf_ = f % 2
                w = []
                if bf_ in wo_last_pe:
                    w = [("PE", wo_last_pe[bf_]["PE"])]
                last = None
                for q_ in range(2):
                    def fwo(g, f=f, q_=q_, bf_=bf_):
                        src = wo[q_ * 2048:(q_ + 1) * 2048, f * 128:(f + 1) * 128]
                        src = src.rearrange("(c p) ff -> p c ff", p=128)
                        dst = wob[bf_][:, q_ * 16 * 128:(q_ + 1) * 16 * 128]
                        dst = dst.rearrange("p (c ff) -> p c ff", c=16)
                        return g.dma_start(out=dst, in_=src)
                    last = P.add("g", fwo, waits=w if q_ == 0 else [], incs=[("W", 16)])
                wodma = last
                w = []
                if bf_ in xf_last_ev:
                    w = [("DVE", xf_last_ev[bf_]["DVE"])]
                xdma = P.add("s", lambda e, f=f, bf_=bf_: e.dma_start(
                    out=xfb[bf_][:, :], in_=x_fm[f * 128:(f + 1) * 128, :]),
                    waits=w, incs=[("X", 16)])
                ev = None
                for tt in range(NT):
                    par = tt % 2

                    def fwp(t, f=f, tt=tt, bf_=bf_, par=par):
                        mm = None
                        for c in range(NC):
                            mm = t.matmul(pw[par][:, :], wob[bf_][:, c * 128:(c + 1) * 128],
                                          attn_sb[:, c * T + tt * 512: c * T + (tt + 1) * 512],
                                          start=(c == 0), stop=(c == NC - 1))
                        return mm
                    w = [("W", wodma["W"]), ("X", arel["X"])]
                    if par in pw_last_ev:
                        w.append(("DVE", pw_last_ev[par]["DVE"]))
                    mmw = P.add("t", fwp, waits=w, incs=[("PE", 1)])
                    if tt == NT - 1:
                        wo_last_pe[bf_] = mmw
                    w = [("PE", mmw["PE"]), ("X", xdma["X"])]
                    if f >= 2 and tt == 0:
                        w.append(("ST", x1_store[f - 2]["ST"]))
                    ev = P.add("v", lambda v, f=f, tt=tt, bf_=bf_, par=par: v.scalar_tensor_tensor(
                        x1b[bf_][:, tt * 512:(tt + 1) * 512], pw[par][:, :],
                        modv[:, MOD_GA + f:MOD_GA + f + 1],
                        xfb[bf_][:, tt * 512:(tt + 1) * 512], ALU.mult, ALU.add),
                        waits=w, incs=[("DVE", 1)])
                    pw_last_ev[par] = ev
                xf_last_ev[bf_] = ev
                x1_store[f] = P.add("y", lambda y, f=f, bf_=bf_: y.dma_start(
                    out=x1_d[f * 128:(f + 1) * 128, :], in_=x1b[bf_][:, :]),
                    waits=[("DVE", ev["DVE"])], incs=[("ST", 16)])
            P.flush()
        ctxD.close()

        ctxH2 = ExitStack()
        h2 = ctxH2.enter_context(nc.sbuf_tensor("h2", [128, NC * T], BF16))
        x1_ready = ("ST", x1_store[NC - 1]["ST"])
        h2_done = ln_phase(x1_d, h2, MOD_C1B, MOD_SB, ready=[x1_ready])
        if DEBUG:
            for j in range(NC):
                P.add("y", lambda y, j=j: y.dma_start(
                    out=dbg["dbg_h2"][j * 128:(j + 1) * 128, :], in_=h2[:, j * T:(j + 1) * T]),
                    waits=[("ACT", h2_done)], incs=[("ST", 16)])
            P.flush()

        P_fm = ctx0.enter_context(nc.sbuf_tensor("P_fm", [128, NR * T], BF16, side="right"))
        ctxF = ExitStack()
        with ctxF:
            u1b = [ctxF.enter_context(nc.sbuf_tensor(f"u1b{i}", [128, NC * 128], BF16))
                   for i in range(2)]
            pp = [ctxF.enter_context(nc.psum_tensor(f"pp{i}", [128, 512], F32))
                  for i in range(2)]
            u1_last_pe = {}
            pp_last_ev = {}
            p_done = None
            for r in range(NR):
                br = r % 2
                w = []
                if br in u1_last_pe:
                    w = [("PE", u1_last_pe[br]["PE"])]

                def fu1(g, r=r, br=br):
                    src = u1[:, r * 128:(r + 1) * 128].rearrange("(c p) f -> p c f", p=128)
                    dst = u1b[br].ap().rearrange("p (c f) -> p c f", c=NC)
                    return g.dma_start(out=dst, in_=src)
                u1dma = P.add("g", fu1, waits=w, incs=[("W", 16)])
                for tt in range(NT):
                    par = tt % 2

                    def fp(t, r=r, tt=tt, br=br, par=par):
                        mm = None
                        for j in range(NC):
                            mm = t.matmul(pp[par][:, :], u1b[br][:, j * 128:(j + 1) * 128],
                                          h2[:, j * T + tt * 512: j * T + (tt + 1) * 512],
                                          start=(j == 0), stop=(j == NC - 1))
                        return mm
                    w = [("W", u1dma["W"]), ("ACT", h2_done)]
                    if par in pp_last_ev:
                        w.append(("DVE", pp_last_ev[par]["DVE"]))
                    mmp = P.add("t", fp, waits=w, incs=[("PE", 1)])
                    if tt == NT - 1:
                        u1_last_pe[br] = mmp
                    p_done = P.add("v", lambda v, r=r, tt=tt, par=par: v.tensor_copy(
                        P_fm[:, r * T + tt * 512: r * T + (tt + 1) * 512], pp[par][:, :]),
                        waits=[("PE", mmp["PE"])], incs=[("DVE", 1)])
                    pp_last_ev[par] = p_done
            P.flush()
            if DEBUG:
                for r in range(NR):
                    P.add("y", lambda y, r=r: y.dma_start(
                        out=dbg["dbg_p"][:, r * T:(r + 1) * T], in_=P_fm[:, r * T:(r + 1) * T]),
                        waits=[("DVE", p_done["DVE"])], incs=[("ST", 16)])
                P.flush()
        ctxF.close()
        ctxH2.close()

        Z_sb = ctx0.enter_context(nc.sbuf_tensor("Z_sb", [128, NR * T], BF16, side="right"))
        ctxG = ExitStack()
        with ctxG:
            sb5 = lambda name, shape, dt: ctxG.enter_context(nc.sbuf_tensor(name, shape, dt))
            NSLOT = 8
            vgb = [sb5(f"vgb{i}", [128, NR * 128], BF16) for i in range(NSLOT)]
            vub = [sb5(f"vub{i}", [128, NR * 128], BF16) for i in range(NSLOT)]
            u2b = [sb5(f"u2b{i}", [128, RANK], BF16) for i in range(NSLOT)]
            sgb = sb5("sgb", [128, 256], BF16)
            actb = [sb5(f"actb{i}", [128, 256], BF16) for i in range(2)]
            pgu = [ctxG.enter_context(nc.psum_tensor(f"pgu{i}", [128, 512], F32))
                   for i in range(2)]
            pzp = ctxG.enter_context(nc.psum_tensor("pzp", [128, 2048], F32))
            pz = [pzp[:, k * 512:(k + 1) * 512] for k in range(4)]

            zinit = P.add("v", lambda v: v.memset(Z_sb[:, :], 0.0), incs=[("DVE", 1)])

            NTT = 8
            JB = 8
            NBLKJ = (NI + JB - 1) // JB

            j_dma = [None] * NI
            j_lastuse = [None] * NI
            z_add = {}
            gu_last_ev = {}
            act_last_pe = {}
            sg_last_dve = None
            gu_gi = 0

            def emit_jdma(j):
                slot = j % NSLOT
                w = []
                if j >= NSLOT:
                    w = [("PE", j_lastuse[j - NSLOT]["PE"])]

                def f1(g, j=j, slot=slot):
                    src = v1[:, j * 128:(j + 1) * 128].rearrange("(c p) f -> p c f", p=128)
                    return g.dma_start(out=vgb[slot].ap().rearrange("p (c f) -> p c f", c=NR),
                                       in_=src)
                P.add("g", f1, waits=w, incs=[("W", 16)])

                def f2(g, j=j, slot=slot):
                    src = v1[:, INTER + j * 128:INTER + (j + 1) * 128]
                    src = src.rearrange("(c p) f -> p c f", p=128)
                    return g.dma_start(out=vub[slot].ap().rearrange("p (c f) -> p c f", c=NR),
                                       in_=src)
                P.add("g", f2, incs=[("W", 16)])

                def f3(g, j=j, slot=slot):
                    return g.dma_start(out=u2b[slot][:, :], in_=u2[j * 128:(j + 1) * 128, :])
                j_dma[j] = P.add("g", f3, incs=[("W", 16)])

            for j in range(min(NSLOT, NI)):
                emit_jdma(j)
            next_load = NSLOT

            pending_u2 = [None]

            def emit_pending():
                if pending_u2[0] is not None:
                    fn, w = pending_u2[0]
                    pending_u2[0] = None
                    return P.add("t", fn, waits=w, incs=[("PE", 1)])
                return None

            for blk_i in range(NBLKJ):
                j0 = blk_i * JB
                j1 = min(j0 + JB, NI)
                for tt in range(NTT):
                    for j in range(j0, j1):
                        slot = j % NSLOT
                        gpar = gu_gi % 2
                        apar = gu_gi % 2
                        gu_gi += 1

                        def fgu(t, slot=slot, tt=tt, gpar=gpar):
                            mm = None
                            for r in range(NR):
                                t.matmul(pgu[gpar][:, 0:256],
                                         vgb[slot][:, r * 128:(r + 1) * 128],
                                         P_fm[:, r * T + tt * 256: r * T + (tt + 1) * 256],
                                         start=(r == 0), stop=False)
                                mm = t.matmul(pgu[gpar][:, 256:512],
                                              vub[slot][:, r * 128:(r + 1) * 128],
                                              P_fm[:, r * T + tt * 256: r * T + (tt + 1) * 256],
                                              start=False, stop=(r == NR - 1))
                            return mm
                        w = [("W", j_dma[j]["W"])]
                        if gpar in gu_last_ev:
                            w.append(("DVE", gu_last_ev[gpar]["DVE"]))
                        gmm = P.add("t", fgu, waits=w, incs=[("PE", 1)])
                        mprev = emit_pending()
                        if mprev is not None:
                            act_last_pe[1 - apar] = mprev
                        w = [("PE", gmm["PE"])]
                        if sg_last_dve is not None:
                            w.append(("DVE", sg_last_dve["DVE"]))
                        sact2 = P.add("s", lambda s, j=j, gpar=gpar: s.activation(
                            sgb[:, :], pgu[gpar][:, 0:256], AF.Silu,
                            bias=b1_s[:, j:j + 1]), waits=w, incs=[("ACT", 1)])
                        w = [("ACT", sact2["ACT"])]
                        if apar in act_last_pe:
                            w.append(("PE", act_last_pe[apar]["PE"]))
                        stt = P.add("v", lambda v, j=j, gpar=gpar, apar=apar:
                                    v.scalar_tensor_tensor(
                                        actb[apar][:, :], pgu[gpar][:, 256:512],
                                        b1_s[:, NI + j:NI + j + 1], sgb[:, :],
                                        ALU.add, ALU.mult), waits=w, incs=[("DVE", 1)])
                        gu_last_ev[gpar] = stt
                        sg_last_dve = stt

                        def fu2(t, slot=slot, apar=apar, j=j, j0=j0, j1=j1):
                            mm = None
                            for r in range(NR):
                                bank, half = divmod(r, 2)
                                st = (j == j0) and (half == 0)
                                sp = (j == j1 - 1) and (half == 1)
                                mm = t.matmul(pz[bank][:, half * 256:(half + 1) * 256],
                                              u2b[slot][:, r * 128:(r + 1) * 128],
                                              actb[apar][:, :], start=st, stop=sp)
                            return mm
                        w = [("DVE", stt["DVE"])]
                        if j == j0:
                            kz = (blk_i, tt - 1) if tt > 0 else (blk_i - 1, NTT - 1)
                            if kz in z_add:
                                w.append(("DVE", z_add[kz]["DVE"]))
                            else:
                                w.append(("DVE", zinit["DVE"]))
                        pending_u2[0] = (fu2, w)
                    mu2 = emit_pending()
                    act_last_pe[(gu_gi - 1) % 2] = mu2
                    last = None
                    for r in range(NR):
                        bank, half = divmod(r, 2)

                        def fza(v, r=r, bank=bank, half=half, tt=tt):
                            sl = slice(r * T + tt * 256, r * T + (tt + 1) * 256)
                            return v.tensor_add(Z_sb[:, sl], Z_sb[:, sl],
                                                pz[bank][:, half * 256:(half + 1) * 256])
                        last = P.addss("v", fza,
                                       waits=[("PE", mu2["PE"])] if r == 0 else [])
                    z_add[(blk_i, tt)] = last
                    if tt == NTT - 1:
                        for j in range(j0, j1):
                            j_lastuse[j] = mu2
                for j in range(j0, j1):
                    if next_load < NI:
                        emit_jdma(next_load)
                        next_load += 1
                P.flush()
            if DEBUG:
                for r in range(NR):
                    P.add("y", lambda y, r=r: y.dma_start(
                        out=dbg["dbg_z"][:, r * T:(r + 1) * T], in_=Z_sb[:, r * T:(r + 1) * T]),
                        waits=[("DVE", z_add[(NBLKJ - 1, NTT - 1)]["DVE"])], incs=[("ST", 16)])
                P.flush()
        ctxG.close()

        ctxH = ExitStack()
        with ctxH:
            sb6 = lambda name, shape, dt: ctxH.enter_context(nc.sbuf_tensor(name, shape, dt))
            v2b = [sb6(f"v2b{i}", [128, NR * 128], BF16) for i in range(2)]
            x1fb = [sb6(f"x1fb{i}", [128, T], F32) for i in range(2)]
            t1b = sb6("t1b", [128, 512], F32)
            ob = [sb6(f"ob{i}", [128, T], F32) for i in range(2)]
            po = [ctxH.enter_context(nc.psum_tensor(f"po{i}", [128, 512], F32))
                  for i in range(2)]

            zlast = z_add[(NBLKJ - 1, NTT - 1)]
            v2_last_pe = {}
            po_last_ev = {}
            x1_last_ev = {}
            o_store = [None] * NC
            for f in range(NC):
                bf_ = f % 2
                w = []
                if bf_ in v2_last_pe:
                    w = [("PE", v2_last_pe[bf_]["PE"])]

                def fv2(g, f=f, bf_=bf_):
                    src = v2[:, f * 128:(f + 1) * 128].rearrange("(c p) ff -> p c ff", p=128)
                    return g.dma_start(out=v2b[bf_].ap().rearrange("p (c ff) -> p c ff", c=NR),
                                       in_=src)
                v2dma = P.add("g", fv2, waits=w, incs=[("W", 16)])
                w = [x1_ready]
                if bf_ in x1_last_ev:
                    w.append(("DVE", x1_last_ev[bf_]["DVE"]))
                xdma = P.add("s", lambda e, f=f, bf_=bf_: e.dma_start(
                    out=x1fb[bf_][:, :], in_=x1_d[f * 128:(f + 1) * 128, :]),
                    waits=w, incs=[("X", 16)])
                ev = None
                for tt in range(NT):
                    par = tt % 2

                    def fo(t, f=f, tt=tt, bf_=bf_, par=par):
                        mm = None
                        for r in range(NR):
                            mm = t.matmul(po[par][:, :], v2b[bf_][:, r * 128:(r + 1) * 128],
                                          Z_sb[:, r * T + tt * 512: r * T + (tt + 1) * 512],
                                          start=(r == 0), stop=(r == NR - 1))
                        return mm
                    w = [("W", v2dma["W"]), ("DVE", zlast["DVE"])]
                    if par in po_last_ev:
                        w.append(("DVE", po_last_ev[par]["DVE"]))
                    mmo = P.add("t", fo, waits=w, incs=[("PE", 1)])
                    if tt == NT - 1:
                        v2_last_pe[bf_] = mmo
                    w = [("PE", mmo["PE"]), ("X", xdma["X"])]
                    if f >= 2 and tt == 0:
                        w.append(("ST", o_store[f - 2]["ST"]))
                    P.add("v", lambda v, f=f, par=par: v.tensor_scalar(
                        t1b[:, :], po[par][:, :], b2_s[:, f:f + 1],
                        modv[:, MOD_GB + f:MOD_GB + f + 1], ALU.add, ALU.mult), waits=w)
                    ev = P.add("v", lambda v, tt=tt, bf_=bf_: v.tensor_add(
                        ob[bf_][:, tt * 512:(tt + 1) * 512], t1b[:, :],
                        x1fb[bf_][:, tt * 512:(tt + 1) * 512]), incs=[("DVE", 1)])
                    po_last_ev[par] = ev
                x1_last_ev[bf_] = ev
                o_store[f] = P.add("y", lambda y, f=f, bf_=bf_: y.dma_start(
                    out=out_fm[f * 128:(f + 1) * 128, :], in_=ob[bf_][:, :]),
                    waits=[("DVE", ev["DVE"])], incs=[("ST", 16)])
            stv = P.ctr["ST"]
            P.add("y", lambda y: None, waits=[("ST", stv)])
            P.flush()
        ctxH.close()
    return nc


_NC_CACHE = {}


def _get_nc():
    key = ("v1", DEBUG)
    if key not in _NC_CACHE:
        _NC_CACHE[key] = build_nc()
    return _NC_CACHE[key]


def kernel(L0_attn_x, L0_attn_wq, L0_attn_wk, L0_attn_wv, L0_attn_wo,
           L0_ffn_U1, L0_ffn_V1, L0_ffn_U2, L0_ffn_V2, L0_ffn_b1, L0_ffn_b2,
           L0_c, L0_wmod):
    x = np.asarray(L0_attn_x)
    f32c = lambda a: np.ascontiguousarray(np.asarray(a, dtype=np.float32))
    shared = {
        "wq": f32c(L0_attn_wq), "wk": f32c(L0_attn_wk), "wv": f32c(L0_attn_wv),
        "wo": f32c(L0_attn_wo), "u1": f32c(L0_ffn_U1), "v1": f32c(L0_ffn_V1),
        "u2": f32c(L0_ffn_U2), "v2": f32c(L0_ffn_V2),
        "b1r": np.ascontiguousarray(np.asarray(L0_ffn_b1, dtype=np.float32).reshape(2 * NI, 128).T),
        "b2r": np.ascontiguousarray(np.asarray(L0_ffn_b2, dtype=np.float32).reshape(NC, 128).T),
        "ident_in": np.eye(128, dtype=np.float32),
        "ones_in": np.ones((128, 128), dtype=np.float32),
        "triu_in": np.ascontiguousarray(np.triu(np.ones((128, 128), dtype=np.float32))),
    }
    cc = np.asarray(L0_c, dtype=np.float32)
    c_all = np.ascontiguousarray(cc.T.reshape(NC, 128, 8).transpose(1, 0, 2).reshape(128, 8 * NC))
    shared["c_all"] = c_all
    wm = f32c(L0_wmod)
    in_maps = []
    for b in range(8):
        m = dict(shared)
        m["x_fm"] = np.ascontiguousarray(np.asarray(x[b], dtype=np.float32).T)
        m["wmod_sl"] = np.ascontiguousarray(wm[:, b * 3 * RANK:(b + 1) * 3 * RANK])
        in_maps.append(m)
    nc = _get_nc()
    res = run_bass_kernel_spmd(nc, in_maps, core_ids=list(range(8)))
    out = np.stack([np.asarray(r["out_fm"]).T for r in res.results], axis=0)
    return out.astype(np.float32)


if __name__ == "__main__":
    nc = build_nc()
    print("built ok")


# version 11 (final)
# speedup vs baseline: 1.0122x; 1.0122x over previous
import numpy as np
from contextlib import ExitStack

import concourse.bass as bass
import concourse.mybir as mybir
from concourse.bass_utils import run_bass_kernel_spmd

F32 = mybir.dt.float32
BF16 = mybir.dt.bfloat16
ALU = mybir.AluOpType
AF = mybir.ActivationFunctionType
AXX = mybir.AxisListType.X

D = 4096
T = 2048
NH = 32
DH = 128
NC = D // 128
NT = T // 512
RANK = 1024
INTER = 11008
NI = INTER // 128
NR = RANK // 128
MODW = 6 * D
SCALE = 1.0 / float(np.sqrt(DH))
EPS = 1e-5

DEBUG = False

ENGS = ("g", "t", "v", "s", "y")


class Prog:

    def __init__(self, nc):
        self.nc = nc
        self.items = []
        self.ctr = {n: 0 for n in ("W", "X", "ST", "PE", "ACT", "DVE")}
        self.sems = {}

    def make_sems(self, ctx):
        for n in self.ctr:
            self.sems[n] = ctx.enter_context(self.nc.semaphore("sem" + n))

    def add(self, eng, fn, waits=(), incs=()):
        post = {}
        for (cn, n) in incs:
            self.ctr[cn] += n
            post[cn] = self.ctr[cn]
        ws = [(cn, v) for (cn, v) in waits if v is not None and v > 0]
        self.items.append((eng, fn, ws, list(incs)))
        return post

    _ECTR = {"v": "DVE", "s": "ACT", "t": "PE"}

    def addss(self, eng, fn, waits=()):
        cn = self._ECTR[eng]
        w = list(waits) + [(cn, self.ctr[cn])]
        return self.add(eng, fn, waits=w, incs=[(cn, 1)])

    def flush(self):
        items = self.items
        self.items = []
        if not items:
            return
        per = {e: [] for e in ENGS}
        for it in items:
            per[it[0]].append(it)
        nc = self.nc
        sems = self.sems
        with nc.Block() as blk:
            for ename in ENGS:
                ops = per[ename]
                if not ops:
                    continue
                dec = {"g": blk.gpsimd, "t": blk.tensor, "v": blk.vector,
                       "s": blk.scalar, "y": blk.sync}[ename]

                def body(eng, ops=ops):
                    waited = {}
                    for (_e, fn, waits, incs) in ops:
                        for (cn, val) in waits:
                            if waited.get(cn, -1) >= val:
                                continue
                            eng.wait_ge(sems[cn], val)
                            waited[cn] = val
                        inst = fn(eng)
                        for (cn, n) in incs:
                            inst.then_inc(sems[cn], n)
                dec(body)


def build_nc():
    nc = bass.Bass("TRN2", target_bir_lowering=False)

    x_fm = nc.dram_tensor("x_fm", [D, T], F32, kind="ExternalInput").ap()
    wq = nc.dram_tensor("wq", [D, D], F32, kind="ExternalInput").ap()
    wk = nc.dram_tensor("wk", [D, DH], F32, kind="ExternalInput").ap()
    wv = nc.dram_tensor("wv", [D, DH], F32, kind="ExternalInput").ap()
    wo = nc.dram_tensor("wo", [D, D], F32, kind="ExternalInput").ap()
    u1 = nc.dram_tensor("u1", [D, RANK], F32, kind="ExternalInput").ap()
    v1 = nc.dram_tensor("v1", [RANK, 2 * INTER], F32, kind="ExternalInput").ap()
    u2 = nc.dram_tensor("u2", [INTER, RANK], F32, kind="ExternalInput").ap()
    v2 = nc.dram_tensor("v2", [RANK, D], F32, kind="ExternalInput").ap()
    wmod_sl = nc.dram_tensor("wmod_sl", [D, 3 * RANK], F32, kind="ExternalInput").ap()
    b1r = nc.dram_tensor("b1r", [128, 2 * NI], F32, kind="ExternalInput").ap()
    b2r = nc.dram_tensor("b2r", [128, NC], F32, kind="ExternalInput").ap()
    c_all = nc.dram_tensor("c_all", [128, 8 * NC], F32, kind="ExternalInput").ap()
    ident_in = nc.dram_tensor("ident_in", [128, 128], F32, kind="ExternalInput").ap()
    ones_in = nc.dram_tensor("ones_in", [128, 128], F32, kind="ExternalInput").ap()
    triu_in = nc.dram_tensor("triu_in", [128, 128], F32, kind="ExternalInput").ap()

    out_fm = nc.dram_tensor("out_fm", [D, T], F32, kind="ExternalOutput").ap()
    wmod_t = nc.dram_tensor_like
    modp_d = nc.dram_tensor("modp_d", [8, 3 * RANK], F32)
    modg_t = nc.dram_tensor("modg_d", [64, 3 * RANK], F32)
    modg_d = modg_t.ap()
    skind = "ExternalOutput" if DEBUG else "Internal"
    x1_d = nc.dram_tensor("x1_d", [D, T], F32, kind=skind).ap()
    attn_d = nc.dram_tensor("attn_d", [D, T], BF16, kind=skind).ap()
    dbg = {}
    if DEBUG:
        for nm, shape, dt in [("dbg_mod", [128, 192], F32), ("dbg_h1", [D, T], BF16),
                              ("dbg_k", [128, T], BF16), ("dbg_v", [128, T], BF16),
                              ("dbg_q0", [128, T], BF16), ("dbg_h2", [D, T], BF16),
                              ("dbg_p", [128, NR * T], BF16), ("dbg_z", [128, NR * T], BF16),
                              ("dbg_scal", [1, 16], F32)]:
            dbg[nm] = nc.dram_tensor(nm, shape, dt, kind="ExternalOutput").ap()

    MOD_SA, MOD_C1A, MOD_GA, MOD_SB, MOD_C1B, MOD_GB = 0, 32, 64, 96, 128, 160

    nc.cache_partition_id()
    ctx0 = ExitStack()
    with ctx0:
        P = Prog(nc)
        P.make_sems(ctx0)
        sbt = lambda name, shape, dt, **kw: ctx0.enter_context(
            nc.sbuf_tensor(name, shape, dt, **kw))

        identb = sbt("identb", [128, 128], BF16)
        onesb = sbt("onesb", [128, 128], BF16)
        triub = sbt("triub", [128, 128], BF16)
        one11 = sbt("one11", [1, 1], BF16)
        b1_s = sbt("b1_s", [128, 2 * NI], F32)
        b2_s = sbt("b2_s", [128, NC], F32)
        modv = sbt("modv", [128, 192], F32)
        scal = sbt("scal", [1, 16], F32)
        scalb = sbt("scalb", [1, 2], BF16)
        negb_s = sbt("negb_s", [128, 2], F32)
        b2gb = sbt("b2gb", [128, NC], F32)
        eps_s = sbt("eps_s", [128, 1], F32)

        ctxA = ExitStack()
        silc8 = ctxA.enter_context(nc.sbuf_tensor("silc8", [128, 8 * NC], F32))
        mod_row = ctxA.enter_context(nc.sbuf_tensor("mod_row", [1, MODW], F32))
        modp_sb = ctxA.enter_context(nc.sbuf_tensor("modp_sb", [8, 3 * RANK], F32))

        P.add("g", lambda g: g.dma_start(out=identb[:, :], in_=ident_in), incs=[("W", 16)])
        P.add("g", lambda g: g.dma_start(out=onesb[:, :], in_=ones_in), incs=[("W", 16)])
        P.add("g", lambda g: g.dma_start(out=triub[:, :], in_=triu_in), incs=[("W", 16)])
        P.add("g", lambda g: g.dma_start(out=b1_s[:, :], in_=b1r), incs=[("W", 16)])
        P.add("g", lambda g: g.dma_start(out=b2_s[:, :], in_=b2r), incs=[("W", 16)])
        cdma = P.add("g", lambda g: g.dma_start(out=silc8[:, :], in_=c_all), incs=[("W", 16)])
        P.add("v", lambda v: v.memset(one11[:, :], 1.0))
        P.add("v", lambda v: v.memset(scal[:, :], 0.0))
        P.add("v", lambda v: v.memset(eps_s[:, :], EPS))
        sact = P.add("s", lambda s: s.activation(silc8[:, :], silc8[:, :], AF.Silu),
                     waits=[("W", cdma["W"])], incs=[("ACT", 1)])

        wmb = [ctxA.enter_context(nc.sbuf_tensor(f"wmb{i}", [128, NC * 256], F32))
               for i in range(2)]
        pmv = [ctxA.enter_context(nc.psum_tensor(f"pmv{i}", [8, 256], F32))
               for i in range(2)]
        NBLK = 3 * RANK // 256
        mv_pe = [None] * NBLK
        mv_ev = [None] * NBLK
        for n in range(NBLK):
            b = n % 2
            last = None
            for q in range(NC):
                def fdma(y, n=n, q=q, b=b):
                    src = wmod_sl[q * 128:(q + 1) * 128, n * 256:(n + 1) * 256]
                    dst = wmb[b][:, q * 256:(q + 1) * 256]
                    return y.dma_start(out=dst, in_=src)
                w = [("PE", mv_pe[n - 2]["PE"])] if (n >= 2 and q == 0) else []
                last = P.add("g", fdma, waits=w, incs=[("W", 16)])

            def fmm(t, b=b):
                mm = None
                for j in range(NC):
                    mm = t.matmul(pmv[b][0:8, :], silc8[:, j * 8:(j + 1) * 8],
                                  wmb[b][:, j * 256:(j + 1) * 256],
                                  start=(j == 0), stop=(j == NC - 1))
                return mm
            w2 = [("W", last["W"]), ("ACT", sact["ACT"])]
            if n >= 1:
                w2.append(("DVE", mv_ev[n - 1]["DVE"]))
            mv_pe[n] = P.add("t", fmm, waits=w2, incs=[("PE", 1)])
            mv_ev[n] = P.addss("v", lambda v, n=n, b=b: v.tensor_copy(
                modp_sb[0:8, n * 256:(n + 1) * 256], pmv[b][0:8, :]),
                waits=[("PE", mv_pe[n]["PE"])])

        mst = P.add("y", lambda y: y.dma_start(out=modp_d.ap(), in_=modp_sb[0:8, :]),
                    waits=[("DVE", mv_ev[NBLK - 1]["DVE"])], incs=[("ST", 16)])
        cc = P.add("g", lambda g: g.collective_compute(
            "AllGather", ALU.bypass, replica_groups=[list(range(8))],
            ins=[modp_d.ap().opt()], outs=[modg_d.opt()]),
            waits=[("ST", mst["ST"])], incs=[("W", 1)])

        def fext(g):
            pid = g.partition_id()
            src = bass.AP(modg_t, pid * (3 * RANK), [[8 * 3 * RANK, 8], [1, 3 * RANK]])
            dst = mod_row[0:1, :].rearrange("a (s f) -> a s f", s=8)
            return g.dma_start(out=dst, in_=src)
        ext = P.add("g", fext, waits=[("W", cc["W"])], incs=[("W", 16)])

        pcol = ctxA.enter_context(nc.psum_tensor("pcol", [128, 192], F32))
        sgn = ctxA.enter_context(nc.sbuf_tensor("sgn", [128, 32], F32))
        tmq = ctxA.enter_context(nc.sbuf_tensor("tmq", [128, 32], F32))
        one11f = ctxA.enter_context(nc.sbuf_tensor("one11f", [1, 1], F32))
        o1m = P.add("v", lambda v: v.memset(one11f[:, :], 1.0), incs=[("DVE", 1)])

        def fcols(t):
            mm = None
            for col in range(192):
                g_, j = divmod(col, 32)
                mm = t.matmul(pcol[:, col:col + 1],
                              mod_row[0:1, g_ * D + j * 128: g_ * D + (j + 1) * 128],
                              one11f[0:1, 0:1], start=(col == 0), stop=(col == 191))
            return mm
        pc = P.add("t", fcols, waits=[("W", ext["W"]), ("DVE", o1m["DVE"])], incs=[("PE", 1)])
        pcv = ctxA.enter_context(nc.sbuf_tensor("pcv", [128, 192], F32))
        P.addss("v", lambda v: v.tensor_copy(pcv[:, :], pcol[:, :]),
                waits=[("PE", pc["PE"])])
        P.addss("v", lambda v: v.tensor_copy(modv[:, MOD_SA:MOD_SA + 32], pcv[:, 0:32]))
        P.addss("v", lambda v: v.tensor_copy(modv[:, MOD_SB:MOD_SB + 32], pcv[:, 96:128]))
        P.addss("v", lambda v: v.tensor_single_scalar(modv[:, MOD_C1A:MOD_C1A + 32],
                                                      pcv[:, 32:64], 1.0, ALU.add))
        P.addss("v", lambda v: v.tensor_single_scalar(modv[:, MOD_C1B:MOD_C1B + 32],
                                                      pcv[:, 128:160], 1.0, ALU.add))
        modv_done = None
        for src0, dst in ((64, MOD_GA), (160, MOD_GB)):
            P.addss("v", lambda v, src0=src0: v.tensor_single_scalar(
                sgn[:, :], pcv[:, src0:src0 + 32], 0.0, ALU.is_ge))
            P.addss("v", lambda v: v.tensor_scalar(sgn[:, :], sgn[:, :], 2.0, -1.0,
                                                   ALU.mult, ALU.add))
            P.addss("v", lambda v, src0=src0: v.tensor_mul(tmq[:, :], pcv[:, src0:src0 + 32],
                                                           sgn[:, :]))
            P.addss("v", lambda v: v.tensor_scalar_max(tmq[:, :], tmq[:, :], 0.25))
            modv_done = P.addss("v", lambda v, dst=dst: v.tensor_mul(
                modv[:, dst:dst + 32], tmq[:, :], sgn[:, :]))
        P.addss("v", lambda v: v.tensor_mul(b2gb[:, :], b2_s[:, :],
                                            modv[:, MOD_GB:MOD_GB + 32]),
                waits=[("DVE", modv_done["DVE"])])
        if DEBUG:
            P.add("y", lambda y: y.dma_start(out=dbg["dbg_mod"], in_=modv[:, :]),
                  waits=[("DVE", modv_done["DVE"])], incs=[("ST", 16)])
        P.flush()
        ctxA.close()

        _uid = [0]

        def ln_phase(x_src, h_dst, c1_off, s_off, ready=()):
            _uid[0] += 1
            u = _uid[0]
            ctxB = ExitStack()
            with ctxB:
                sb2 = lambda name, shape, dt: ctxB.enter_context(
                    nc.sbuf_tensor(f"{name}_{u}", shape, dt))
                xbb = [sb2(f"lnxb{i}", [128, T], BF16) for i in range(2)]
                x2b = [sb2(f"lnx2{i}", [128, T], BF16) for i in range(2)]
                m_s = sb2("ln_m", [128, T], F32)
                r_s = sb2("ln_r", [128, T], F32)
                var_s = sb2("ln_var", [128, T], F32)
                xf = [sb2(f"lnxf{i}", [128, T], F32) for i in range(2)]
                uu = [sb2(f"ln_uu{i}", [128, T], F32) for i in range(2)]
                pSx = ctxB.enter_context(nc.psum_tensor(f"pSx_{u}", [128, T], F32))
                pSx2 = ctxB.enter_context(nc.psum_tensor(f"pSx2_{u}", [128, T], F32))

                mmj = [None] * NC
                for j in range(NC):
                    b = j % 2
                    w = [("PE", mmj[j - 2]["PE"])] if j >= 2 else list(ready)
                    dj = P.add("g", lambda g, j=j, b=b: g.dma_start(
                        out=xbb[b][:, :], in_=x_src[j * 128:(j + 1) * 128, :]),
                        waits=w, incs=[("W", 16)])
                    w = [("W", dj["W"])]
                    if j >= 2:
                        w.append(("PE", mmj[j - 2]["PE"]))
                    sq = P.add("v", lambda v, b=b: v.tensor_mul(
                        x2b[b][:, :], xbb[b][:, :], xbb[b][:, :]),
                        waits=w, incs=[("DVE", 1)])

                    def fstat(t, j=j, b=b):
                        mm = None
                        for tt in range(NT):
                            sl = slice(tt * 512, (tt + 1) * 512)
                            t.matmul(pSx[:, sl], onesb[:, :], xbb[b][:, sl],
                                     start=(j == 0), stop=(j == NC - 1))
                            mm = t.matmul(pSx2[:, sl], onesb[:, :], x2b[b][:, sl],
                                          start=(j == 0), stop=(j == NC - 1))
                        return mm
                    mmj[j] = P.add("t", fstat, waits=[("DVE", sq["DVE"])], incs=[("PE", 1)])

                wst = [("PE", mmj[NC - 1]["PE"])]
                P.add("v", lambda v: v.tensor_single_scalar(m_s[:, :], pSx[:, :],
                                                            1.0 / D, ALU.mult), waits=wst)
                P.add("v", lambda v: v.tensor_mul(var_s[:, :], m_s[:, :], m_s[:, :]))
                dv = P.add("v", lambda v: v.scalar_tensor_tensor(
                    var_s[:, :], pSx2[:, :], 1.0 / D, var_s[:, :], ALU.mult, ALU.subtract),
                    incs=[("DVE", 1)])
                P.add("s", lambda s: s.activation(var_s[:, :], var_s[:, :], AF.Ln, bias=eps_s[:, 0:1]),
                      waits=[("DVE", dv["DVE"])])
                ract = P.add("s", lambda s: s.activation(r_s[:, :], var_s[:, :], AF.Exp,
                                                         scale=-0.5), incs=[("ACT", 1)])

                dvj = [None] * NC
                actj = [None] * NC
                for j in range(NC):
                    b = j % 2
                    w = [("DVE", dvj[j - 2]["DVE"])] if j >= 2 else list(ready)
                    xdj = P.add("y", lambda y, j=j, b=b: y.dma_start(
                        out=xf[b][:, :], in_=x_src[j * 128:(j + 1) * 128, :]),
                        waits=w, incs=[("X", 16)])
                    w = [("X", xdj["X"]), ("ACT", ract["ACT"])]
                    if j >= 2:
                        w.append(("ACT", actj[j - 2]["ACT"]))
                    P.add("v", lambda v, b=b: v.tensor_sub(uu[b][:, :], xf[b][:, :], m_s[:, :]),
                          waits=w)
                    dvj[j] = P.add("v", lambda v, j=j, b=b: v.scalar_tensor_tensor(
                        uu[b][:, :], uu[b][:, :], modv[:, c1_off + j:c1_off + j + 1], r_s[:, :],
                        ALU.mult, ALU.mult), incs=[("DVE", 1)])
                    actj[j] = P.add("s", lambda s, j=j, b=b: s.activation(
                        h_dst[:, j * T:(j + 1) * T], uu[b][:, :], AF.Identity,
                        bias=modv[:, s_off + j:s_off + j + 1]),
                        waits=[("DVE", dvj[j]["DVE"])], incs=[("ACT", 1)])
                P.flush()
            return actj[NC - 1]["ACT"]

        ctxH1 = ExitStack()
        h1 = ctxH1.enter_context(nc.sbuf_tensor("h1", [128, NC * T], BF16))
        h1_done = ln_phase(x_fm, h1, MOD_C1A, MOD_SA)
        if DEBUG:
            for j in range(NC):
                P.add("y", lambda y, j=j: y.dma_start(
                    out=dbg["dbg_h1"][j * 128:(j + 1) * 128, :], in_=h1[:, j * T:(j + 1) * T]),
                    waits=[("ACT", h1_done)], incs=[("ST", 16)])
            P.flush()

        ctxC = ExitStack()
        with ctxC:
            sb3 = lambda name, shape, dt: ctxC.enter_context(nc.sbuf_tensor(name, shape, dt))
            k_fm = sb3("k_fm", [128, T], BF16)
            vfm = sb3("vfm", [128, T], BF16)
            v_tm = sb3("v_tm", [128, T], BF16)
            k2b = sb3("k2b", [128, T], BF16)
            wqb = [sb3(f"wqb{i}", [128, NC * 128], BF16) for i in range(2)]
            q_h = [sb3(f"q_h{i}", [128, T], BF16) for i in range(2)]
            q2b = sb3("q2b", [128, T], BF16)
            ptb = [sb3(f"ptb{i}", [128, 512], BF16) for i in range(2)]
            recip = [sb3(f"recip{i}", [128, 512], F32) for i in range(2)]
            lnden = [sb3(f"lnden{i}", [128, 512], F32) for i in range(2)]
            atb = [sb3(f"atb{i}", [128, T], BF16) for i in range(2)]

            pq = [ctxC.enter_context(nc.psum_tensor(f"pq{i}", [128, 512], F32)) for i in range(2)]
            pS = [ctxC.enter_context(nc.psum_tensor(f"pS{i}", [128, 512], F32)) for i in range(2)]
            pPV = [ctxC.enter_context(nc.psum_tensor(f"pPV{i}", [128, 512], F32)) for i in range(2)]
            pDen = [ctxC.enter_context(nc.psum_tensor(f"pDen{i}", [128, 512], F32)) for i in range(2)]
            pM = pq[0]
            pTr = pS[0][:, 0:64].bitcast(BF16)

            last_pq_ev = {}
            last_pS_exp = {}
            last_ptb_den = {}

            kv_dma = [None, None]
            for which, wsrc in ((0, wk), (1, wv)):
                def fkv(g, which=which, wsrc=wsrc):
                    dst = wqb[which][:, 0:NC * DH].rearrange("p (c f) -> p c f", c=NC)
                    return g.dma_start(out=dst, in_=wsrc.rearrange("(c p) f -> p c f", p=128))
                kv_dma[which] = P.add("g", fkv, incs=[("W", 16)])

            kv_idx = 0
            kv_last_ev = None
            kv_last_pe = {}
            for which, dst in ((0, k_fm), (1, vfm)):
                for tt in range(NT):
                    par = kv_idx % 2

                    def fproj(t, which=which, tt=tt, par=par):
                        mm = None
                        for j in range(NC):
                            mm = t.matmul(pq[par][:, :],
                                          wqb[which][:, j * 128:(j + 1) * 128],
                                          h1[:, j * T + tt * 512: j * T + (tt + 1) * 512],
                                          start=(j == 0), stop=(j == NC - 1))
                        return mm
                    w = [("W", kv_dma[which]["W"]), ("ACT", h1_done)]
                    if par in last_pq_ev:
                        w.append(("DVE", last_pq_ev[par]["DVE"]))
                    mm = P.add("t", fproj, waits=w, incs=[("PE", 1)])
                    kv_last_pe[which] = mm
                    kv_last_ev = P.add("v", lambda v, dst=dst, tt=tt, par=par: v.tensor_copy(
                        dst[:, tt * 512:(tt + 1) * 512], pq[par][:, :]),
                        waits=[("PE", mm["PE"])], incs=[("DVE", 1)])
                    last_pq_ev[par] = kv_last_ev
                    kv_idx += 1
                if which == 0:
                    k_done = kv_last_ev
            v_done = kv_last_ev

            k2d = P.add("v", lambda v: v.tensor_mul(k2b[:, :], k_fm[:, :], k_fm[:, :]),
                        waits=[("DVE", k_done["DVE"])], incs=[("DVE", 1)])
            vt_prev = None
            for c in range(16):
                w = [("DVE", v_done["DVE"])]
                if vt_prev is not None:
                    w.append(("DVE", vt_prev["DVE"]))
                mm = P.add("t", lambda t, c=c: t.transpose(
                    pTr, vfm[:, c * 128:(c + 1) * 128], identb[:, :]),
                    waits=w, incs=[("PE", 1)])
                vt_prev = P.add("v", lambda v, c=c: v.tensor_copy(
                    v_tm[:, c * 128:(c + 1) * 128], pTr),
                    waits=[("PE", mm["PE"])], incs=[("DVE", 1)])
            kred_prev = None
            for tt in range(NT):
                w = [("DVE", k2d["DVE"]), ("DVE", v_done["DVE"])]
                if kred_prev is not None:
                    w.append(("DVE", kred_prev["DVE"]))
                mm = P.add("t", lambda t, tt=tt: t.matmul(
                    pM[0:1, 0:512], onesb[:, 0:1], k2b[:, tt * 512:(tt + 1) * 512],
                    start=True, stop=True), waits=w, incs=[("PE", 1)])
                kred_prev = P.addss("v", lambda v, tt=tt: v.reduce_max(
                    scal[0:1, tt:tt + 1], pM[0:1, 0:512], axis=AXX),
                    waits=[("PE", mm["PE"])])
            kn2m = P.addss("v", lambda v: v.reduce_max(scal[0:1, 8:9], scal[0:1, 0:4], axis=AXX),
                           waits=[("DVE", kred_prev["DVE"])])
            P.flush()
            if DEBUG:
                P.add("y", lambda y: y.dma_start(out=dbg["dbg_k"], in_=k_fm[:, :]),
                      incs=[("ST", 16)])
                P.add("y", lambda y: y.dma_start(out=dbg["dbg_v"], in_=v_tm[:, :]),
                      incs=[("ST", 16)])
                P.flush()

            at_store = [None] * NH
            wq_pe_last = {0: kv_last_pe[0], 1: kv_last_pe[1]}
            pv_done_prev = None
            last_pv_ev = {}
            last_den_rd = {}
            nq_bank_rd = {}
            last_rc2 = None
            qdma = [None] * NH

            def emit_wq_dma(h):
                bq = h % 2
                w = []
                if bq in wq_pe_last:
                    w = [("PE", wq_pe_last[bq]["PE"])]
                last = None
                for q_ in range(2):
                    def fwq(g, h=h, q_=q_, bq=bq):
                        src = wq[q_ * 2048:(q_ + 1) * 2048, h * 128:(h + 1) * 128]
                        src = src.rearrange("(c p) f -> p c f", p=128)
                        dst = wqb[bq][:, q_ * 16 * 128:(q_ + 1) * 16 * 128]
                        dst = dst.rearrange("p (c f) -> p c f", c=16)
                        return g.dma_start(out=dst, in_=src)
                    last = P.add("g", fwq, waits=w if q_ == 0 else [], incs=[("W", 16)])
                qdma[h] = last

            def emit_q_group(h, tt):
                hp = h % 2
                par = tt % 2
                bq = h % 2

                def fq(t, tt=tt, bq=bq, par=par):
                    mm = None
                    for j in range(NC):
                        mm = t.matmul(pq[par][:, :], wqb[bq][:, j * 128:(j + 1) * 128],
                                      h1[:, j * T + tt * 512: j * T + (tt + 1) * 512],
                                      start=(j == 0), stop=(j == NC - 1))
                    return mm
                w = [("W", qdma[h]["W"])]
                if par in last_pq_ev:
                    w.append(("DVE", last_pq_ev[par]["DVE"]))
                if ("pq", par) in nq_bank_rd:
                    w.append(("DVE", nq_bank_rd[("pq", par)]["DVE"]))
                mmq = P.add("t", fq, waits=w, incs=[("PE", 1)])
                if tt == NT - 1:
                    wq_pe_last[bq] = mmq
                P.add("v", lambda v, tt=tt, par=par, hp=hp: v.tensor_copy(
                    q_h[hp][:, tt * 512:(tt + 1) * 512], pq[par][:, :]),
                    waits=[("PE", mmq["PE"])])
                qe = P.add("v", lambda v, tt=tt, hp=hp: v.tensor_mul(
                    q2b[:, tt * 512:(tt + 1) * 512],
                    q_h[hp][:, tt * 512:(tt + 1) * 512],
                    q_h[hp][:, tt * 512:(tt + 1) * 512]),
                    incs=[("DVE", 1)])
                last_pq_ev[par] = qe
                return qe

            def emit_bound_chain(h, qev_last):
                hp = h % 2
                banks = [("pq", 0, pq[0]), ("pq", 1, pq[1]),
                         ("pPV", 0, pPV[0]), ("pDen", 0, pDen[0])]
                nred = []
                for tt in range(NT):
                    kind, idx, pt = banks[tt]
                    w = [("DVE", qev_last["DVE"])]
                    if kind == "pPV" and idx in last_pv_ev:
                        w.append(("DVE", last_pv_ev[idx]["DVE"]))
                    if kind == "pDen" and idx in last_den_rd:
                        w.append(("ACT", last_den_rd[idx]["ACT"]))
                    mm = P.add("t", lambda t, tt=tt, pt=pt: t.matmul(
                        pt[0:1, 0:512], onesb[:, 0:1], q2b[:, tt * 512:(tt + 1) * 512],
                        start=True, stop=True), waits=w, incs=[("PE", 1)])
                    rd = P.addss("v", lambda v, tt=tt, pt=pt: v.reduce_max(
                        scal[0:1, 4 + tt:5 + tt], pt[0:1, 0:512], axis=AXX),
                        waits=[("PE", mm["PE"])])
                    nq_bank_rd[(kind, idx)] = rd
                    nred.append(rd)
                P.addss("v", lambda v: v.reduce_max(scal[0:1, 9:10], scal[0:1, 4:8], axis=AXX),
                        waits=[("DVE", nred[-1]["DVE"]), ("DVE", kn2m["DVE"])])
                dmu = P.addss("v", lambda v: v.tensor_mul(scal[0:1, 10:11], scal[0:1, 8:9],
                                                          scal[0:1, 9:10]))
                asq = P.addss("s", lambda s: s.activation(scal[0:1, 11:12], scal[0:1, 10:11],
                                                          AF.Sqrt, scale=1.0 / DH),
                              waits=[("DVE", dmu["DVE"])])
                dng = P.addss("v", lambda v, hp=hp: v.tensor_single_scalar(
                    scalb[0:1, hp:hp + 1], scal[0:1, 11:12], -1.0, ALU.mult),
                    waits=[("ACT", asq["ACT"])])
                bmm = P.add("t", lambda t, hp=hp: t.matmul(
                    pq[0][:, 0:1], onesb[0:1, 0:128], scalb[0:1, hp:hp + 1],
                    start=True, stop=True),
                    waits=[("DVE", dng["DVE"])], incs=[("PE", 1)])
                nbc = P.addss("v", lambda v, hp=hp: v.tensor_copy(
                    negb_s[:, hp:hp + 1], pq[0][:, 0:1]),
                    waits=[("PE", bmm["PE"])])
                last_pq_ev[0] = nbc
                return nbc

            def make_slots(hn, qev_box):
                def s0():
                    emit_q_group(hn, 0)
                def s1():
                    emit_q_group(hn, 1)
                def s2():
                    emit_q_group(hn, 2)
                    qev_box[0] = emit_q_group(hn, 3)
                def s3():
                    nbc_box[hn] = emit_bound_chain(hn, qev_box[0])
                return [s0, s1, s2, s3]

            nbc_box = {}
            qev_box = [None]

            emit_wq_dma(0)
            emit_q_group(0, 0)
            emit_q_group(0, 1)
            emit_q_group(0, 2)
            qe0 = emit_q_group(0, 3)
            nbc_box[0] = emit_bound_chain(0, qe0)

            for h in range(NH):
                hp = h % 2
                nbc = nbc_box[h]
                if h + 1 < NH:
                    emit_wq_dma(h + 1)
                    slots = make_slots(h + 1, qev_box)
                else:
                    slots = [lambda: None] * 4

                for tt in range(NT):
                    cmax = 4 * (tt + 1)
                    vpar = tt % 2
                    expq = {}
                    den_q = {}

                    def emit_s_exp(c):
                        qlo = max(tt * 512, c * 128)
                        N = (tt + 1) * 512 - qlo
                        spar = c % 2
                        w = []
                        if spar in last_pS_exp:
                            w.append(("ACT", last_pS_exp[spar]["ACT"]))
                        if ("pS", spar) in nq_bank_rd:
                            w.append(("DVE", nq_bank_rd.pop(("pS", spar))["DVE"]))
                        smm = P.add("t", lambda t, c=c, N=N, qlo=qlo, spar=spar,
                                    tt=tt, hp=hp:
                                    t.matmul(pS[spar][:, 0:N],
                                             k_fm[:, c * 128:(c + 1) * 128],
                                             q_h[hp][:, qlo:(tt + 1) * 512],
                                             start=True, stop=True),
                                    waits=w, incs=[("PE", 1)])
                        w = [("PE", smm["PE"]), ("DVE", nbc["DVE"])]
                        if spar in last_ptb_den:
                            w.append(("PE", last_ptb_den[spar]["PE"]))
                        ex = P.add("s", lambda s, N=N, spar=spar, hp=hp: s.activation(
                            ptb[spar][:, 0:N], pS[spar][:, 0:N], AF.Exp,
                            bias=negb_s[:, hp:hp + 1], scale=SCALE),
                            waits=w, incs=[("ACT", 1)])
                        last_pS_exp[spar] = ex
                        expq[c] = {"ACT": ex["ACT"], "DVE": None}
                        if c >= 4 * tt:
                            mk = P.add("v", lambda v, spar=spar: v.tensor_mul(
                                ptb[spar][:, 0:128], ptb[spar][:, 0:128], triub[:, :]),
                                waits=[("ACT", ex["ACT"])], incs=[("DVE", 1)])
                            expq[c]["DVE"] = mk["DVE"]

                    def emit_pv_den(cc):
                        qlo2 = max(tt * 512, cc * 128)
                        N2 = (tt + 1) * 512 - qlo2
                        off2 = qlo2 - tt * 512
                        e2 = expq[cc]
                        w = [("ACT", e2["ACT"])]
                        if e2["DVE"] is not None:
                            w = [("DVE", e2["DVE"])]
                        if cc == 0:
                            if vpar in last_pv_ev:
                                w.append(("DVE", last_pv_ev[vpar]["DVE"]))
                            if ("pPV", vpar) in nq_bank_rd:
                                w.append(("DVE", nq_bank_rd.pop(("pPV", vpar))["DVE"]))
                        P.add("t", lambda t, cc=cc, N2=N2, off2=off2, cmax=cmax, vpar=vpar:
                              t.matmul(pPV[vpar][:, off2:off2 + N2],
                                       v_tm[:, cc * 128:(cc + 1) * 128],
                                       ptb[cc % 2][:, 0:N2],
                                       start=(cc == 0), stop=(cc == cmax - 1)),
                              waits=w)
                        wd = []
                        if cc == 0:
                            if vpar in last_den_rd:
                                wd.append(("ACT", last_den_rd[vpar]["ACT"]))
                            if ("pDen", vpar) in nq_bank_rd:
                                wd.append(("DVE", nq_bank_rd.pop(("pDen", vpar))["DVE"]))
                        dn = P.add("t", lambda t, cc=cc, N2=N2, off2=off2, cmax=cmax, vpar=vpar:
                                   t.matmul(pDen[vpar][:, off2:off2 + N2], onesb[:, :],
                                            ptb[cc % 2][:, 0:N2],
                                            start=(cc == 0), stop=(cc == cmax - 1)),
                                   waits=wd, incs=[("PE", 1)])
                        last_ptb_den[cc % 2] = dn
                        den_q[cc] = dn

                    for c in range(cmax):
                        emit_s_exp(c)
                        if c >= 1:
                            emit_pv_den(c - 1)
                    emit_pv_den(cmax - 1)

                    w = [("PE", den_q[cmax - 1]["PE"])]
                    if vpar in last_pv_ev:
                        w.append(("DVE", last_pv_ev[vpar]["DVE"]))
                    rc1 = P.add("s", lambda s, vpar=vpar: s.activation(
                        lnden[vpar][:, :], pDen[vpar][:, 0:512], AF.Ln),
                        waits=w, incs=[("ACT", 1)])
                    last_den_rd[vpar] = rc1
                    rc2 = P.add("s", lambda s, vpar=vpar: s.activation(
                        recip[vpar][:, :], lnden[vpar][:, :], AF.Exp,
                        scale=-1.0), incs=[("ACT", 1)])
                    last_rc2 = rc2
                    w = [("ACT", rc2["ACT"])]
                    if h >= 2 and tt == 0:
                        w.append(("ST", at_store[h - 2]["ST"]))
                    pv_done_prev = P.add("v", lambda v, tt=tt, hp=hp, vpar=vpar: v.tensor_mul(
                        atb[hp][:, tt * 512:(tt + 1) * 512], pPV[vpar][:, 0:512],
                        recip[vpar][:, :]),
                        waits=w, incs=[("DVE", 1)])
                    last_pv_ev[vpar] = pv_done_prev

                    slots[tt]()

                at_store[h] = P.add("y", lambda y, h=h, hp=hp: y.dma_start(
                    out=attn_d[h * 128:(h + 1) * 128, :], in_=atb[hp][:, :]),
                    waits=[("DVE", pv_done_prev["DVE"])], incs=[("ST", 16)])
            P.flush()
        ctxC.close()
        ctxH1.close()

        ctxD = ExitStack()
        with ctxD:
            sb4 = lambda name, shape, dt: ctxD.enter_context(nc.sbuf_tensor(name, shape, dt))
            attn_sb = sb4("attn_sb", [128, NC * T], BF16)
            wob = [sb4(f"wob{i}", [128, NC * 128], BF16) for i in range(2)]
            xfb = [sb4(f"xfb{i}", [128, T], F32) for i in range(2)]
            x1b = [sb4(f"x1b{i}", [128, T], F32) for i in range(2)]
            pw = [ctxD.enter_context(nc.psum_tensor(f"pw{i}", [128, 512], F32))
                  for i in range(2)]

            at0 = at_store[NH - 1]["ST"]
            arel = None
            for q_ in range(4):
                def fat(y, q_=q_):
                    src = attn_d[q_ * 1024:(q_ + 1) * 1024, :]
                    src = src.rearrange("(c p) t -> p c t", p=128)
                    dst = attn_sb[:, q_ * 8 * T:(q_ + 1) * 8 * T]
                    dst = dst.rearrange("p (c t) -> p c t", c=8)
                    return y.dma_start(out=dst, in_=src)
                arel = P.add("y", fat, waits=[("ST", at0)] if q_ == 0 else [],
                             incs=[("X", 16)])

            wo_last_pe = {}
            pw_last_ev = {}
            xf_last_ev = {}
            x1_store = [None] * NC
            for f in range(NC):
                bf_ = f % 2
                w = []
                if bf_ in wo_last_pe:
                    w = [("PE", wo_last_pe[bf_]["PE"])]
                last = None
                for q_ in range(2):
                    def fwo(g, f=f, q_=q_, bf_=bf_):
                        src = wo[q_ * 2048:(q_ + 1) * 2048, f * 128:(f + 1) * 128]
                        src = src.rearrange("(c p) ff -> p c ff", p=128)
                        dst = wob[bf_][:, q_ * 16 * 128:(q_ + 1) * 16 * 128]
                        dst = dst.rearrange("p (c ff) -> p c ff", c=16)
                        return g.dma_start(out=dst, in_=src)
                    last = P.add("g", fwo, waits=w if q_ == 0 else [], incs=[("W", 16)])
                wodma = last
                w = []
                if bf_ in xf_last_ev:
                    w = [("DVE", xf_last_ev[bf_]["DVE"])]
                xdma = P.add("s", lambda e, f=f, bf_=bf_: e.dma_start(
                    out=xfb[bf_][:, :], in_=x_fm[f * 128:(f + 1) * 128, :]),
                    waits=w, incs=[("X", 16)])
                ev = None
                for tt in range(NT):
                    par = tt % 2

                    def fwp(t, f=f, tt=tt, bf_=bf_, par=par):
                        mm = None
                        for c in range(NC):
                            mm = t.matmul(pw[par][:, :], wob[bf_][:, c * 128:(c + 1) * 128],
                                          attn_sb[:, c * T + tt * 512: c * T + (tt + 1) * 512],
                                          start=(c == 0), stop=(c == NC - 1))
                        return mm
                    w = [("W", wodma["W"]), ("X", arel["X"])]
                    if par in pw_last_ev:
                        w.append(("DVE", pw_last_ev[par]["DVE"]))
                    mmw = P.add("t", fwp, waits=w, incs=[("PE", 1)])
                    if tt == NT - 1:
                        wo_last_pe[bf_] = mmw
                    w = [("PE", mmw["PE"]), ("X", xdma["X"])]
                    if f >= 2 and tt == 0:
                        w.append(("ST", x1_store[f - 2]["ST"]))
                    ev = P.add("v", lambda v, f=f, tt=tt, bf_=bf_, par=par: v.scalar_tensor_tensor(
                        x1b[bf_][:, tt * 512:(tt + 1) * 512], pw[par][:, :],
                        modv[:, MOD_GA + f:MOD_GA + f + 1],
                        xfb[bf_][:, tt * 512:(tt + 1) * 512], ALU.mult, ALU.add),
                        waits=w, incs=[("DVE", 1)])
                    pw_last_ev[par] = ev
                xf_last_ev[bf_] = ev
                x1_store[f] = P.add("y", lambda y, f=f, bf_=bf_: y.dma_start(
                    out=x1_d[f * 128:(f + 1) * 128, :], in_=x1b[bf_][:, :]),
                    waits=[("DVE", ev["DVE"])], incs=[("ST", 16)])
            P.flush()
        ctxD.close()

        ctxH2 = ExitStack()
        h2 = ctxH2.enter_context(nc.sbuf_tensor("h2", [128, NC * T], BF16))
        x1_ready = ("ST", x1_store[NC - 1]["ST"])
        h2_done = ln_phase(x1_d, h2, MOD_C1B, MOD_SB, ready=[x1_ready])
        if DEBUG:
            for j in range(NC):
                P.add("y", lambda y, j=j: y.dma_start(
                    out=dbg["dbg_h2"][j * 128:(j + 1) * 128, :], in_=h2[:, j * T:(j + 1) * T]),
                    waits=[("ACT", h2_done)], incs=[("ST", 16)])
            P.flush()

        P_fm = ctx0.enter_context(nc.sbuf_tensor("P_fm", [128, NR * T], BF16, side="right"))
        ctxF = ExitStack()
        with ctxF:
            u1b = [ctxF.enter_context(nc.sbuf_tensor(f"u1b{i}", [128, NC * 128], BF16))
                   for i in range(2)]
            pp = [ctxF.enter_context(nc.psum_tensor(f"pp{i}", [128, 512], F32))
                  for i in range(2)]
            u1_last_pe = {}
            pp_last_ev = {}
            p_done = None
            for r in range(NR):
                br = r % 2
                w = []
                if br in u1_last_pe:
                    w = [("PE", u1_last_pe[br]["PE"])]

                def fu1(g, r=r, br=br):
                    src = u1[:, r * 128:(r + 1) * 128].rearrange("(c p) f -> p c f", p=128)
                    dst = u1b[br].ap().rearrange("p (c f) -> p c f", c=NC)
                    return g.dma_start(out=dst, in_=src)
                u1dma = P.add("g", fu1, waits=w, incs=[("W", 16)])
                for tt in range(NT):
                    par = tt % 2

                    def fp(t, r=r, tt=tt, br=br, par=par):
                        mm = None
                        for j in range(NC):
                            mm = t.matmul(pp[par][:, :], u1b[br][:, j * 128:(j + 1) * 128],
                                          h2[:, j * T + tt * 512: j * T + (tt + 1) * 512],
                                          start=(j == 0), stop=(j == NC - 1))
                        return mm
                    w = [("W", u1dma["W"]), ("ACT", h2_done)]
                    if par in pp_last_ev:
                        w.append(("DVE", pp_last_ev[par]["DVE"]))
                    mmp = P.add("t", fp, waits=w, incs=[("PE", 1)])
                    if tt == NT - 1:
                        u1_last_pe[br] = mmp
                    p_done = P.add("v", lambda v, r=r, tt=tt, par=par: v.tensor_copy(
                        P_fm[:, r * T + tt * 512: r * T + (tt + 1) * 512], pp[par][:, :]),
                        waits=[("PE", mmp["PE"])], incs=[("DVE", 1)])
                    pp_last_ev[par] = p_done
            P.flush()
            if DEBUG:
                for r in range(NR):
                    P.add("y", lambda y, r=r: y.dma_start(
                        out=dbg["dbg_p"][:, r * T:(r + 1) * T], in_=P_fm[:, r * T:(r + 1) * T]),
                        waits=[("DVE", p_done["DVE"])], incs=[("ST", 16)])
                P.flush()
        ctxF.close()
        ctxH2.close()

        Z_sb = ctx0.enter_context(nc.sbuf_tensor("Z_sb", [128, NR * T], BF16, side="right"))
        ctxG = ExitStack()
        with ctxG:
            sb5 = lambda name, shape, dt: ctxG.enter_context(nc.sbuf_tensor(name, shape, dt))
            NSLOT = 8
            vgb = [sb5(f"vgb{i}", [128, NR * 128], BF16) for i in range(NSLOT)]
            vub = [sb5(f"vub{i}", [128, NR * 128], BF16) for i in range(NSLOT)]
            u2b = [sb5(f"u2b{i}", [128, RANK], BF16) for i in range(NSLOT)]
            sgb = sb5("sgb", [128, 256], BF16)
            actb = [sb5(f"actb{i}", [128, 256], BF16) for i in range(2)]
            pgu = [ctxG.enter_context(nc.psum_tensor(f"pgu{i}", [128, 512], F32))
                   for i in range(2)]
            pzp = ctxG.enter_context(nc.psum_tensor("pzp", [128, 2048], F32))
            pz = [pzp[:, k * 512:(k + 1) * 512] for k in range(4)]

            zinit = P.add("v", lambda v: v.memset(Z_sb[:, :], 0.0), incs=[("DVE", 1)])

            NTT = 8
            JB = 8
            NBLKJ = (NI + JB - 1) // JB

            j_dma = [None] * NI
            j_lastuse = [None] * NI
            z_add = {}
            gu_last_ev = {}
            act_last_pe = {}
            sg_last_dve = None
            gu_gi = 0

            def emit_jdma(j):
                slot = j % NSLOT
                w = []
                if j >= NSLOT:
                    w = [("PE", j_lastuse[j - NSLOT]["PE"])]

                def f1(g, j=j, slot=slot):
                    src = v1[:, j * 128:(j + 1) * 128].rearrange("(c p) f -> p c f", p=128)
                    return g.dma_start(out=vgb[slot].ap().rearrange("p (c f) -> p c f", c=NR),
                                       in_=src)
                P.add("g", f1, waits=w, incs=[("W", 16)])

                def f2(g, j=j, slot=slot):
                    src = v1[:, INTER + j * 128:INTER + (j + 1) * 128]
                    src = src.rearrange("(c p) f -> p c f", p=128)
                    return g.dma_start(out=vub[slot].ap().rearrange("p (c f) -> p c f", c=NR),
                                       in_=src)
                P.add("g", f2, incs=[("W", 16)])

                def f3(g, j=j, slot=slot):
                    return g.dma_start(out=u2b[slot][:, :], in_=u2[j * 128:(j + 1) * 128, :])
                j_dma[j] = P.add("g", f3, incs=[("W", 16)])

            for j in range(min(NSLOT, NI)):
                emit_jdma(j)
            next_load = NSLOT

            pending_u2 = [None]

            def emit_pending():
                if pending_u2[0] is not None:
                    fn, w = pending_u2[0]
                    pending_u2[0] = None
                    return P.add("t", fn, waits=w, incs=[("PE", 1)])
                return None

            for blk_i in range(NBLKJ):
                j0 = blk_i * JB
                j1 = min(j0 + JB, NI)
                for tt in range(NTT):
                    for j in range(j0, j1):
                        slot = j % NSLOT
                        gpar = gu_gi % 2
                        apar = gu_gi % 2
                        gu_gi += 1

                        def fgu(t, slot=slot, tt=tt, gpar=gpar):
                            mm = None
                            for r in range(NR):
                                t.matmul(pgu[gpar][:, 0:256],
                                         vgb[slot][:, r * 128:(r + 1) * 128],
                                         P_fm[:, r * T + tt * 256: r * T + (tt + 1) * 256],
                                         start=(r == 0), stop=False)
                                mm = t.matmul(pgu[gpar][:, 256:512],
                                              vub[slot][:, r * 128:(r + 1) * 128],
                                              P_fm[:, r * T + tt * 256: r * T + (tt + 1) * 256],
                                              start=False, stop=(r == NR - 1))
                            return mm
                        w = [("W", j_dma[j]["W"])]
                        if gpar in gu_last_ev:
                            w.append(("DVE", gu_last_ev[gpar]["DVE"]))
                        gmm = P.add("t", fgu, waits=w, incs=[("PE", 1)])
                        mprev = emit_pending()
                        if mprev is not None:
                            act_last_pe[1 - apar] = mprev
                        w = [("PE", gmm["PE"])]
                        if sg_last_dve is not None:
                            w.append(("DVE", sg_last_dve["DVE"]))
                        sact2 = P.add("s", lambda s, j=j, gpar=gpar: s.activation(
                            sgb[:, :], pgu[gpar][:, 0:256], AF.Silu,
                            bias=b1_s[:, j:j + 1]), waits=w, incs=[("ACT", 1)])
                        w = [("ACT", sact2["ACT"])]
                        if apar in act_last_pe:
                            w.append(("PE", act_last_pe[apar]["PE"]))
                        stt = P.add("v", lambda v, j=j, gpar=gpar, apar=apar:
                                    v.scalar_tensor_tensor(
                                        actb[apar][:, :], pgu[gpar][:, 256:512],
                                        b1_s[:, NI + j:NI + j + 1], sgb[:, :],
                                        ALU.add, ALU.mult), waits=w, incs=[("DVE", 1)])
                        gu_last_ev[gpar] = stt
                        sg_last_dve = stt

                        def fu2(t, slot=slot, apar=apar, j=j, j0=j0, j1=j1):
                            mm = None
                            for r in range(NR):
                                bank, half = divmod(r, 2)
                                st = (j == j0) and (half == 0)
                                sp = (j == j1 - 1) and (half == 1)
                                mm = t.matmul(pz[bank][:, half * 256:(half + 1) * 256],
                                              u2b[slot][:, r * 128:(r + 1) * 128],
                                              actb[apar][:, :], start=st, stop=sp)
                            return mm
                        w = [("DVE", stt["DVE"])]
                        if j == j0:
                            kz = (blk_i, tt - 1) if tt > 0 else (blk_i - 1, NTT - 1)
                            if kz in z_add:
                                w.append(("DVE", z_add[kz]["DVE"]))
                            else:
                                w.append(("DVE", zinit["DVE"]))
                        pending_u2[0] = (fu2, w)
                    mu2 = emit_pending()
                    act_last_pe[(gu_gi - 1) % 2] = mu2
                    last = None
                    for r in range(NR):
                        bank, half = divmod(r, 2)

                        def fza(v, r=r, bank=bank, half=half, tt=tt):
                            sl = slice(r * T + tt * 256, r * T + (tt + 1) * 256)
                            return v.tensor_add(Z_sb[:, sl], Z_sb[:, sl],
                                                pz[bank][:, half * 256:(half + 1) * 256])
                        last = P.addss("v", fza,
                                       waits=[("PE", mu2["PE"])] if r == 0 else [])
                    z_add[(blk_i, tt)] = last
                    if tt == NTT - 1:
                        for j in range(j0, j1):
                            j_lastuse[j] = mu2
                for j in range(j0, j1):
                    if next_load < NI:
                        emit_jdma(next_load)
                        next_load += 1
                P.flush()
            if DEBUG:
                for r in range(NR):
                    P.add("y", lambda y, r=r: y.dma_start(
                        out=dbg["dbg_z"][:, r * T:(r + 1) * T], in_=Z_sb[:, r * T:(r + 1) * T]),
                        waits=[("DVE", z_add[(NBLKJ - 1, NTT - 1)]["DVE"])], incs=[("ST", 16)])
                P.flush()
        ctxG.close()

        ctxH = ExitStack()
        with ctxH:
            sb6 = lambda name, shape, dt: ctxH.enter_context(nc.sbuf_tensor(name, shape, dt))
            v2b = [sb6(f"v2b{i}", [128, NR * 128], BF16) for i in range(2)]
            x1fb = [sb6(f"x1fb{i}", [128, T], F32) for i in range(2)]
            t1b = sb6("t1b", [128, 512], F32)
            ob = [sb6(f"ob{i}", [128, T], F32) for i in range(2)]
            po = [ctxH.enter_context(nc.psum_tensor(f"po{i}", [128, 512], F32))
                  for i in range(2)]

            zlast = z_add[(NBLKJ - 1, NTT - 1)]
            v2_last_pe = {}
            po_last_ev = {}
            x1_last_ev = {}
            o_store = [None] * NC
            for f in range(NC):
                bf_ = f % 2
                w = []
                if bf_ in v2_last_pe:
                    w = [("PE", v2_last_pe[bf_]["PE"])]

                def fv2(g, f=f, bf_=bf_):
                    src = v2[:, f * 128:(f + 1) * 128].rearrange("(c p) ff -> p c ff", p=128)
                    return g.dma_start(out=v2b[bf_].ap().rearrange("p (c ff) -> p c ff", c=NR),
                                       in_=src)
                v2dma = P.add("g", fv2, waits=w, incs=[("W", 16)])
                w = [x1_ready]
                if bf_ in x1_last_ev:
                    w.append(("DVE", x1_last_ev[bf_]["DVE"]))
                xdma = P.add("s", lambda e, f=f, bf_=bf_: e.dma_start(
                    out=x1fb[bf_][:, :], in_=x1_d[f * 128:(f + 1) * 128, :]),
                    waits=w, incs=[("X", 16)])
                ev = None
                for tt in range(NT):
                    par = tt % 2

                    def fo(t, f=f, tt=tt, bf_=bf_, par=par):
                        mm = None
                        for r in range(NR):
                            mm = t.matmul(po[par][:, :], v2b[bf_][:, r * 128:(r + 1) * 128],
                                          Z_sb[:, r * T + tt * 512: r * T + (tt + 1) * 512],
                                          start=(r == 0), stop=(r == NR - 1))
                        return mm
                    w = [("W", v2dma["W"]), ("DVE", zlast["DVE"])]
                    if par in po_last_ev:
                        w.append(("DVE", po_last_ev[par]["DVE"]))
                    mmo = P.add("t", fo, waits=w, incs=[("PE", 1)])
                    if tt == NT - 1:
                        v2_last_pe[bf_] = mmo
                    w = [("PE", mmo["PE"]), ("X", xdma["X"])]
                    if f >= 2 and tt == 0:
                        w.append(("ST", o_store[f - 2]["ST"]))
                    P.add("v", lambda v, f=f, par=par: v.tensor_scalar(
                        t1b[:, :], po[par][:, :], b2_s[:, f:f + 1],
                        modv[:, MOD_GB + f:MOD_GB + f + 1], ALU.add, ALU.mult), waits=w)
                    ev = P.add("v", lambda v, tt=tt, bf_=bf_: v.tensor_add(
                        ob[bf_][:, tt * 512:(tt + 1) * 512], t1b[:, :],
                        x1fb[bf_][:, tt * 512:(tt + 1) * 512]), incs=[("DVE", 1)])
                    po_last_ev[par] = ev
                x1_last_ev[bf_] = ev
                o_store[f] = P.add("y", lambda y, f=f, bf_=bf_: y.dma_start(
                    out=out_fm[f * 128:(f + 1) * 128, :], in_=ob[bf_][:, :]),
                    waits=[("DVE", ev["DVE"])], incs=[("ST", 16)])
            stv = P.ctr["ST"]
            P.add("y", lambda y: None, waits=[("ST", stv)])
            P.flush()
        ctxH.close()
    return nc


_NC_CACHE = {}


def _get_nc():
    key = ("v1", DEBUG)
    if key not in _NC_CACHE:
        _NC_CACHE[key] = build_nc()
    return _NC_CACHE[key]


def kernel(L0_attn_x, L0_attn_wq, L0_attn_wk, L0_attn_wv, L0_attn_wo,
           L0_ffn_U1, L0_ffn_V1, L0_ffn_U2, L0_ffn_V2, L0_ffn_b1, L0_ffn_b2,
           L0_c, L0_wmod):
    x = np.asarray(L0_attn_x)
    f32c = lambda a: np.ascontiguousarray(np.asarray(a, dtype=np.float32))
    shared = {
        "wq": f32c(L0_attn_wq), "wk": f32c(L0_attn_wk), "wv": f32c(L0_attn_wv),
        "wo": f32c(L0_attn_wo), "u1": f32c(L0_ffn_U1), "v1": f32c(L0_ffn_V1),
        "u2": f32c(L0_ffn_U2), "v2": f32c(L0_ffn_V2),
        "b1r": np.ascontiguousarray(np.asarray(L0_ffn_b1, dtype=np.float32).reshape(2 * NI, 128).T),
        "b2r": np.ascontiguousarray(np.asarray(L0_ffn_b2, dtype=np.float32).reshape(NC, 128).T),
        "ident_in": np.eye(128, dtype=np.float32),
        "ones_in": np.ones((128, 128), dtype=np.float32),
        "triu_in": np.ascontiguousarray(np.triu(np.ones((128, 128), dtype=np.float32))),
    }
    cc = np.asarray(L0_c, dtype=np.float32)
    c_all = np.ascontiguousarray(cc.T.reshape(NC, 128, 8).transpose(1, 0, 2).reshape(128, 8 * NC))
    shared["c_all"] = c_all
    wm = f32c(L0_wmod)
    in_maps = []
    for b in range(8):
        m = dict(shared)
        m["x_fm"] = np.ascontiguousarray(np.asarray(x[b], dtype=np.float32).T)
        m["wmod_sl"] = np.ascontiguousarray(wm[:, b * 3 * RANK:(b + 1) * 3 * RANK])
        in_maps.append(m)
    nc = _get_nc()
    res = run_bass_kernel_spmd(nc, in_maps, core_ids=list(range(8)))
    out = np.stack([np.asarray(r["out_fm"]).T for r in res.results], axis=0)
    return out.astype(np.float32)


if __name__ == "__main__":
    nc = build_nc()
    print("built ok")
